# version 24
# baseline (speedup 1.0000x reference)
import numpy as np
import concourse.bass as bass
import concourse.mybir as mybir

F32 = mybir.dt.float32
BF16 = mybir.dt.bfloat16
I32 = mybir.dt.int32
U8 = mybir.dt.uint8
AF = mybir.ActivationFunctionType
ALU = mybir.AluOpType
AX = mybir.AxisListType

ENGINES = ("pe", "act", "dve", "pool", "sp")

D = 1024
T = 2048
NCI = 8
NTT = 16
NH = 16
HD = 64
HID = 3072
NHC = 24
EPS = 1e-5
QSCALE = 0.125


class Prog:
    NDMASEM = 8

    def __init__(self, nc):
        self.nc = nc
        self.sem = {e: nc.alloc_semaphore(f"sem_{e}") for e in ENGINES}
        self.cnt = {e: 0 for e in ENGINES}
        self.dma_i = {}
        for q in ("sp", "pool"):
            for r in range(self.NDMASEM):
                key = f"{q}dma{r}"
                self.sem[key] = nc.alloc_semaphore(f"sem_{key}")
                self.cnt[key] = 0
            self.dma_i[q] = 0
        self.ops = {e: [] for e in ENGINES}
        self.waited = {e: {p: 0 for p in self.sem} for e in ENGINES}
        self.n_ops = 0

    def op(self, eng, fn, after=(), inc=True, dma=False):
        waits = []
        if dma:
            q = eng
            slot = self.dma_i[q] % self.NDMASEM
            self.dma_i[q] += 1
            semkey = f"{q}dma{slot}"
            if self.cnt[semkey] > self.waited[eng][semkey]:
                waits.append((self.sem[semkey], self.cnt[semkey]))
                self.waited[eng][semkey] = self.cnt[semkey]
        else:
            semkey = eng
        for tok in after:
            if tok is None:
                continue
            pkey, pval = tok
            if (not dma) and pkey == eng == "pe":
                continue
            if pval > self.waited[eng][pkey]:
                waits.append((self.sem[pkey], pval))
                self.waited[eng][pkey] = pval
        amt = 16 if dma else 1
        tok = None
        if inc:
            self.cnt[semkey] += amt
            tok = (semkey, self.cnt[semkey])
        self.ops[eng].append((waits, fn, (self.sem[semkey], amt) if inc else None))
        self.n_ops += 1
        return tok

    def dma(self, eng, fn, after=()):
        return self.op(eng, fn, after=after, inc=True, dma=True)

    def last(self, eng):
        return (eng, self.cnt[eng])

    def barrier(self):
        nc = self.nc
        if not hasattr(self, "_bar_seen"):
            self._bar_seen = {k: 0 for k in self.cnt}
        for q in ("sp", "pool"):
            waits = []
            for r in range(self.NDMASEM):
                key = f"{q}dma{r}"
                if self.cnt[key] > self._bar_seen[key]:
                    waits.append((self.sem[key], self.cnt[key]))
                self._bar_seen[key] = self.cnt[key]
            if waits:
                self.ops[q].append((waits, None, None))
        with nc.Block() as blk:
            secmap = {"pe": blk.tensor, "act": blk.scalar, "dve": blk.vector,
                      "pool": blk.gpsimd, "sp": blk.sync}
            for e in ENGINES:
                eng_ops = self.ops[e]
                if not eng_ops:
                    continue

                def make(eng_ops=eng_ops):
                    def _(eng):
                        for waits, fn, inc in eng_ops:
                            for (sem, val) in waits:
                                eng.wait_ge(sem, val)
                            if fn is None:
                                continue
                            inst = fn(eng)
                            if inc is not None:
                                inst.then_inc(inc[0], inc[1])
                    return _
                secmap[e](make())
        self.ops = {e: [] for e in ENGINES}
        for e in ENGINES:
            for p in self.sem:
                self.waited[e][p] = self.cnt[p]


class Arena:
    def __init__(self, nc, name, nbytes, align=64):
        self.nc = nc
        self.h = nc.alloc_sbuf_tensor(name, [128, nbytes], U8)
        self.base = nc.lookup_mloc(self.h).addr
        self.off = 0
        self.nbytes = nbytes
        self.align = align
        self.n = 0

    def alloc(self, name, shape, dtype):
        isz = {F32: 4, BF16: 2, I32: 4, U8: 1}[dtype]
        per_part = int(np.prod(shape[1:])) * isz if len(shape) > 1 else isz
        a = self.align
        self.off = (self.off + a - 1) // a * a
        assert self.off + per_part <= self.nbytes, \
            f"arena overflow: {name} needs {per_part} at {self.off}/{self.nbytes}"
        h = self.nc.alloc_sbuf_tensor_at(f"{name}_{self.n}", list(shape), dtype,
                                         offset=self.base + self.off)
        self.n += 1
        self.off += per_part
        return h

    def mark(self):
        return self.off

    def reset(self, off=0):
        self.off = off


class K:

    def __init__(self, n_batch=4, dbg=None):
        self.nc = nc = bass.Bass("TRN2", target_bir_lowering=False)
        self.n_batch = n_batch
        self.dbg = dbg

        ein = lambda n, s, d=F32: nc.dram_tensor(n, s, d, kind="ExternalInput").ap()
        self.x_in = ein("x_in", [n_batch, T, D])
        self.Wq0 = ein("Wq0", [D, D]); self.Wk0 = ein("Wk0", [D, D]); self.Wv0 = ein("Wv0", [D, D])
        self.Wg = ein("Wg", [HID, D], I32); self.Wu = ein("Wu", [HID, D], I32)
        self.Wd = ein("Wd", [D, HID], I32)
        self.Wq1 = ein("Wq1", [D, D]); self.Wkv = ein("Wkv", [D, HD])
        self.nullkv = ein("nullkv", [HD]); self.Wout = ein("Wout", [D, D])
        self.wfr = ein("wfr", [D, D]); self.wfi = ein("wfi", [D, D])
        self.bias1 = ein("bias1", [D])
        self.Ctab = ein("Ctab", [D, D]); self.Stab = ein("Stab", [D, D])
        self.masks_in = ein("masks_in", [128, 128])
        self.ident_in = ein("ident_in", [128, 128])
        self.y_out = nc.dram_tensor("y_out", [n_batch, T, D], F32, kind="ExternalOutput").ap()
        if dbg is not None:
            self.x_dbg = nc.dram_tensor("x_dbg", [n_batch, 128, NCI * T], F32,
                                        kind="ExternalOutput").ap()

        dts = lambda n, s, d=BF16: nc.dram_tensor(n, s, d, kind="Internal").ap()
        self.dWq0 = dts("dWq0", [128, NCI * D]); self.dWk0 = dts("dWk0", [128, NCI * D])
        self.dWv0 = dts("dWv0", [128, NCI * D]); self.dWq1 = dts("dWq1", [128, NCI * D])
        self.dWout = dts("dWout", [128, NCI * D])
        self.dWgT = dts("dWgT", [128, NHC * NCI * 128])
        self.dWuT = dts("dWuT", [128, NHC * NCI * 128])
        self.dWdT = dts("dWdT", [128, NCI * NHC * 128])

        sb = lambda n, s, d: nc.alloc_sbuf_tensor(n, s, d)
        self.x_bf = sb("x_bf", [128, NCI * T], BF16)
        self.M_bf = sb("M_bf", [128, NCI * D], BF16)
        self.sMask = sb("sMask", [128, 128], BF16)
        self.z65 = sb("z65", [128, 65], BF16)
        self.ones128 = sb("ones128", [128, 128], BF16)
        self.onescol = sb("onescol", [128, 1], BF16)
        self.zrow = sb("zrow", [1, 260], BF16)
        self.ident = sb("ident", [128, 128], BF16)
        self.sWkv = sb("sWkv", [128, NCI * HD], BF16)
        self.nk_col = sb("nk_col", [128, 1], BF16)
        self.nkv65 = sb("nkv65", [1, 65], BF16)
        self.sBias = sb("sBias", [128, 8], F32)
        self.stage32 = sb("stage32", [128, 2048], F32)
        self.xtokp = sb("xtokp", [128, 2048], BF16)

        self.cEPS = sb("cEPS", [128, 1], F32)
        nc.const_aps.register(F32, EPS, self.cEPS[:, :])
        self.ps = nc.alloc_psum_tensor("ps", [128, 4096], F32)
        self.ar = Arena(nc, "arena", 144500)
        self.P = Prog(nc)
        self.bank_readers = {i: [] for i in range(8)}
        self.bank_writer = {i: None for i in range(8)}

    def pbank(self, i, n=1):
        return self.ps[:, 512 * i: 512 * (i + n)]

    def pnum(self, par, h, rows=65):
        bi = 4 + 2 * par + h
        return self.ps[0:rows, 512 * bi: 512 * bi + 512]

    def prep_attn_weights(self):
        P, nc = self.P, self.nc
        pairs = [(self.Wq0, self.dWq0), (self.Wk0, self.dWk0), (self.Wv0, self.dWv0),
                 (self.Wq1, self.dWq1), (self.Wout, self.dWout)]
        ar = self.ar
        ar.reset()
        NB = 6
        stg = [ar.alloc(f"wstg{i}", [128, 2048], F32) for i in range(NB)]
        sbf = [ar.alloc(f"wcast{i}", [128, 2048], BF16) for i in range(NB)]
        tcast = {}
        tstore = {}
        j = 0
        for (w, dw) in pairs:
            wr = w.rearrange("(c p) o -> p c o", p=128)
            for q in range(4):
                bi = j % NB
                qn = ("sp", "pool")[j % 2]
                tl = P.dma(qn, lambda e, wr=wr, q=q, bi=bi: e.dma_start(
                    stg[bi][:, :], wr[:, 2 * q:2 * q + 2, :]),
                    after=[tcast.get(bi)])
                tc = P.op("act", lambda e, bi=bi: e.copy(sbf[bi][:, :], stg[bi][:, :]),
                          after=[tl, tstore.get(bi)])
                tcast[bi] = tc
                ts = P.dma(("pool", "sp")[j % 2], lambda e, dw=dw, q=q, bi=bi:
                           e.dma_start(dw[:, 2048 * q:2048 * (q + 1)], sbf[bi][:, :]),
                           after=[tc])
                tstore[bi] = ts
                j += 1
        wr = self.Wkv.rearrange("(c p) o -> p c o", p=128)
        tl = P.dma("sp", lambda e: e.dma_start(self.stage32[:, 0:512], wr[:, :, :]))
        P.op("act", lambda e: e.copy(self.sWkv[:, :], self.stage32[:, 0:512]), after=[tl])
        nk = self.nullkv
        t1 = P.dma("pool", lambda e: e.dma_start(
            self.stage32[0:64, 1024:1025], nk.rearrange("(a b) -> a b", b=1)))
        t2 = P.dma("pool", lambda e: e.dma_start(
            self.stage32[64:128, 1024:1025], nk.rearrange("(a b) -> a b", b=1)), after=[t1])
        t3 = P.dma("pool", lambda e: e.dma_start(
            self.stage32[0:1, 1040:1104], nk.rearrange("(a b) -> a b", a=1)), after=[t2])
        tc1 = P.op("dve", lambda e: e.tensor_copy(self.nk_col[:, :], self.stage32[:, 1024:1025]), after=[t1, t2])
        tc2 = P.op("dve", lambda e: e.tensor_copy(self.nkv65[0:1, 0:64], self.stage32[0:1, 1040:1104]), after=[t3])
        P.op("pool", lambda e: e.memset(self.nkv65[0:1, 64:65], 1.0), inc=False)
        t4 = P.dma("pool", lambda e: e.dma_start(
            self.sBias[:, :], self.bias1.rearrange("(c p) -> p c", p=128),
            allow_slow_non_contiguous=True), after=[tc1, tc2])
        t5 = P.dma("sp", lambda e: e.dma_start(self.stage32[:, 0:128], self.masks_in[:, :]),
                   after=[tc1, tc2, P.last("act")])
        tMsk = P.op("dve", lambda e: e.tensor_copy(self.sMask[:, :], self.stage32[:, 0:128]),
                    after=[t5])
        tz = P.op("pool", lambda e: e.memset(self.z65[:, 0:64], 0.0))
        P.op("pool", lambda e: e.memset(self.z65[:, 64:65], 1.0), after=[tz])
        t6 = P.dma("pool", lambda e: e.dma_start(self.xtokp[:, 0:256].bitcast(F32),
                                                 self.ident_in[:, :]), after=[t4])
        P.op("dve", lambda e: e.tensor_copy(self.ident[:, :],
                                            self.xtokp[:, 0:256].bitcast(F32)), after=[t6])
        P.op("pool", lambda e: e.memset(self.cEPS[:, :], EPS), inc=False)
        P.op("pool", lambda e: e.memset(self.ones128[:, :], 1.0), inc=False)
        P.op("pool", lambda e: e.memset(self.onescol[:, :], 1.0), inc=False)
        P.op("pool", lambda e: e.memset(self.zrow[:, :], 0.0), inc=False)
        P.barrier()

    def prep_ffn_weights(self, barrier=True):
        P, nc, ar = self.P, self.nc, self.ar
        ar.reset()
        NB = 4
        sI = [ar.alloc(f"ffnI{i}", [128, 1024], I32) for i in range(NB)]
        sB = [ar.alloc(f"ffnB{i}", [128, 1024], BF16) for i in range(NB)]
        sRow = [ar.alloc(f"ffnR{i}", [128, 1024], BF16) for i in range(2)]
        ptr = [self.ps[:, 512 * i:512 * i + 64].bitcast(BF16) for i in (4, 5)]
        jobs = []
        for (w, dwT) in ((self.Wg, self.dWgT), (self.Wu, self.dWuT)):
            for r in range(HID // 128):
                jobs.append((w[128 * r:128 * r + 128, :],
                             dwT[:, 1024 * r:1024 * r + 1024]))
        for r in range(D // 128):
            for cc in range(3):
                jobs.append((self.Wd[128 * r:128 * r + 128, 1024 * cc:1024 * cc + 1024],
                             self.dWdT[:, 3072 * r + 1024 * cc:3072 * r + 1024 * cc + 1024]))
        tcast = {}
        tuse = {}
        tstore = {}
        tevac = {}
        for j, (src, dst) in enumerate(jobs):
            bi = j % NB
            ri = j % 2
            tl = P.dma(("sp", "pool")[j % 2], lambda e, src=src, bi=bi: e.dma_start(
                sI[bi][:, :], src), after=[tcast.get(bi)])
            tc = P.op("act", lambda e, bi=bi: e.activation(
                sB[bi][:, :], sI[bi][:, :], AF.Copy, scale=1.0 / 127.0),
                after=[tl, tuse.get(bi)])
            tcast[bi] = tc
            evs = []
            for c in range(8):
                tpi = (j * 8 + c) % 2
                tt = P.op("pe", lambda e, bi=bi, c=c, tpi=tpi: e.transpose(
                    ptr[tpi][:, 0:128], sB[bi][:, 128 * c:128 * c + 128],
                    self.ident[:, :]), after=[tc, tevac.get(tpi)])
                tuse[bi] = tt
                te = P.op("dve", lambda e, ri=ri, c=c, tpi=tpi: e.tensor_copy(
                    sRow[ri][:, 128 * c:128 * c + 128], ptr[tpi][:, 0:128]),
                    after=[tt, tstore.get(ri) if c == 0 else None])
                tevac[tpi] = te
                evs.append(te)
            ts = P.dma(("pool", "sp")[j % 2], lambda e, dst=dst, ri=ri: e.dma_start(
                dst, sRow[ri][:, :]), after=evs)
            tstore[ri] = ts
        if barrier:
            P.barrier()

    def prep_M(self, barrier=True, ar_off=0):
        P, nc, ar = self.P, self.nc, self.ar
        ar.reset(ar_off)
        sC = ar.alloc("sC", [128, NCI * D], BF16)
        sS = ar.alloc("sS", [128, NCI * D], BF16)
        sWr = ar.alloc("sWr", [128, NCI * D], BF16)
        sWi = ar.alloc("sWi", [128, NCI * D], BF16)
        sT1r = ar.alloc("sT1r", [128, NCI * D], BF16)
        sT1i = ar.alloc("sT1i", [128, NCI * D], BF16)
        self._sT1r = sT1r; self._sT1i = sT1i; self._sC = sC; self._sS = sS; self._sWr = sWr
        stg = [ar.alloc(f"mstage{i}", [128, 2048], F32) for i in range(2)]
        loads = [(self.Ctab, [(sC, 1.0)]), (self.Stab, [(sS, 1.0)]),
                 (self.wfr, [(sWr, 1.0)]), (self.wfi, [(sWi, 1.0)])]
        tcast = {}
        i = 0
        for (src, dsts) in loads:
            sr = src.rearrange("(c p) o -> p c o", p=128)
            for q in range(4):
                pi = i % 2
                tl = P.dma(("sp", "pool")[i % 2], lambda e, sr=sr, q=q, pi=pi: e.dma_start(
                    stg[pi][:, :], sr[:, 2 * q:2 * q + 2, :]), after=[tcast.get(pi)])
                tprev = None
                for (dst, sc) in dsts:
                    tprev = P.op("act", lambda e, dst=dst, q=q, sc=sc, pi=pi: e.activation(
                        dst[:, 2048 * q:2048 * (q + 1)], stg[pi][:, :], AF.Copy, scale=sc),
                        after=[tl, tprev])
                tcast[pi] = tprev
                i += 1
        load_toks = [tcast[0], tcast[1]]

        chain_guard = {}

        def mm_chain(dst, terms, scale, deps):
            tprev_ev = chain_guard
            evacs = []
            for oc in range(NCI):
                for mt in range(2):
                    bank = (oc * 2 + mt) % 2
                    pb = self.pbank(bank)
                    tmm = None
                    n_mm = len(terms) * NCI
                    imm = 0
                    for (lh, rh) in terms:
                        for ci in range(NCI):
                            first = (imm == 0)
                            last = (imm == n_mm - 1)
                            tmm = P.op("pe", lambda e, lh=lh, rh=rh, ci=ci, oc=oc, mt=mt,
                                       first=first, last=last, pb=pb: e.matmul(
                                pb[:, :],
                                lh[:, D * ci + 128 * oc: D * ci + 128 * oc + 128],
                                rh[:, D * ci + 512 * mt: D * ci + 512 * mt + 512],
                                start=first, stop=last),
                                after=[tprev_ev.get(bank)] + (deps if first else []))
                            imm += 1
                    te = P.op("act", lambda e, dst=dst, oc=oc, mt=mt, pb=pb, scale=scale:
                              e.activation(dst[:, D * oc + 512 * mt: D * oc + 512 * mt + 512],
                                           pb[:, :], AF.Copy, scale=scale), after=[tmm])
                    tprev_ev[bank] = te
                    evacs.append(te)
            return evacs

        ev1 = mm_chain(sT1r, [(sWr, sC), (sWi, sS)], 1.0, load_toks)
        tneg = []
        for q in range(4):
            tneg.append(P.op("act", lambda e, q=q: e.activation(
                sS[:, 2048 * q:2048 * q + 2048], sS[:, 2048 * q:2048 * q + 2048],
                AF.Copy, scale=-1.0), after=ev1 if q == 0 else []))
        ev1 += mm_chain(sT1i, [(sWi, sC), (sWr, sS)], 1.0, load_toks + tneg)
        mm_chain(self.M_bf, [(sT1r, sC), (sT1i, sS)], 1.0 / D, ev1)
        if barrier:
            P.barrier()

    def load_x(self, b, src=None, feature_major_src=False, barrier=True):
        P, ar = self.P, self.ar
        src = self.x_in if src is None else src
        if feature_major_src:
            for q in range(8):
                tl = P.dma("sp", lambda e, q=q: e.dma_start(
                    self.stage32[:, :], src[b, :, 2048 * q:2048 * (q + 1)]))
                P.op("dve", lambda e, q=q: e.tensor_copy(
                    self.x_bf[:, 2048 * q:2048 * (q + 1)], self.stage32[:, :]), after=[tl])
            P.barrier()
            return
        ptr = [self.ps[:, 512 * i:512 * i + 64].bitcast(BF16) for i in (6, 7)]
        tcast = {}
        ttr = {}
        tev = {}
        for tt in range(NTT):
            pi = tt % 2
            tl = P.dma(("sp", "pool")[pi], lambda e, tt=tt, pi=pi: e.dma_start(
                self.stage32[:, 1024 * pi:1024 * pi + 1024],
                src[b, 128 * tt:128 * tt + 128, :]), after=[tcast.get(pi)])
            tc = P.op("act", lambda e, tt=tt, pi=pi: e.copy(
                self.xtokp[:, 1024 * pi:1024 * pi + 1024],
                self.stage32[:, 1024 * pi:1024 * pi + 1024]),
                after=[tl, ttr.get(pi)])
            tcast[pi] = tc
            for ci in range(NCI):
                tpi = (tt * NCI + ci) % 2
                t1 = P.op("pe", lambda e, pi=pi, ci=ci, tpi=tpi: e.transpose(
                    ptr[tpi][:, 0:128],
                    self.xtokp[:, 1024 * pi + 128 * ci: 1024 * pi + 128 * ci + 128],
                    self.ident[:, :]), after=[tc, tev.get(tpi)])
                ttr[pi] = t1
                tev[tpi] = P.op("dve", lambda e, tt=tt, ci=ci, tpi=tpi: e.tensor_copy(
                    self.x_bf[:, T * ci + 128 * tt: T * ci + 128 * tt + 128],
                    ptr[tpi][:, 0:128]), after=[t1])
        if barrier:
            P.barrier()

    def dump_x(self, b):
        P = self.P
        for q in range(8):
            tc = P.op("act", lambda e, q=q: e.copy(
                self.stage32[:, :], self.x_bf[:, 2048 * q:2048 * (q + 1)]))
            P.dma("sp", lambda e, q=q: e.dma_start(
                self.x_dbg[b, :, 2048 * q:2048 * (q + 1)], self.stage32[:, :]), after=[tc])
            P.barrier()

    def emit_proj(self, sW, evac, n_out_chunks=NCI, trange=(0, T), scale=None):
        P = self.P
        t0, t1 = trange
        ntq = (t1 - t0) // 512
        tprev = {}
        for co in range(n_out_chunks):
            for tq in range(ntq):
                bank = (co * ntq + tq) % 2
                pb = self.pbank(bank)
                tmm = None
                for ci in range(NCI):
                    tmm = P.op("pe", lambda e, ci=ci, co=co, tq=tq, pb=pb: e.matmul(
                        pb[:, :],
                        sW[:, D * ci + 128 * co: D * ci + 128 * co + 128],
                        self.x_bf[:, T * ci + t0 + 512 * tq: T * ci + t0 + 512 * tq + 512],
                        start=(ci == 0), stop=(ci == NCI - 1)),
                        after=[tprev.get(bank)])
                tprev[bank] = evac(co, tq, pb, tmm)

    def l0_attn(self, b):
        P, ar, nc = self.P, self.ar, self.nc
        ar.reset()
        sQ = ar.alloc("sQ", [128, NCI * T], BF16)
        sK = ar.alloc("sK", [128, NCI * T], BF16)
        sV = ar.alloc("sV", [128, NTT * NH * 65], BF16)
        bufA = ar.alloc("wbufA", [128, NCI * D], BF16)
        bufB = ar.alloc("wbufB", [128, NCI * D], BF16)
        sPT = [ar.alloc(f"sPT2{p}", [128, 1024], BF16) for p in range(2)]
        sPM2 = [ar.alloc(f"sPM0m{i}", [128, 128], BF16) for i in range(4)]
        nsb = ar.alloc("nsb0", [128, 512], F32)
        num_sb = [nsb[0:64, :], nsb[64:128, :]]
        rden_sb = [ar.alloc(f"rd0{h}", [1, 512], BF16) for h in range(2)]
        self._attn_common(b, sQ, sK, sV, bufA, bufB, sPT, sPM2,
                          num_sb, rden_sb)
        ar.reset()
        self._ln(b)

    def _attn_common(self, b, sQ, sK, sV, bufA, bufB, sPT, sPM2,
                     num_sb, rden_sb):
        P = self.P
        tq_done = self._proj_qkv_l0(bufA, bufB, sQ, sK, sV)
        tones = P.op("pool", lambda e: e.memset(
            sV.ap().rearrange("p (n c) -> p n c", c=65)[:, :, 64:65], 1.0), after=[tq_done])
        P.barrier()

        prs = [(0, 64), (64, 128)]
        pt_readers = {p: [] for p in range(2)}
        exp_tok = {}
        pm_guard = {}
        bank_guard = {}
        num_guard = {}
        rden_guard = {}
        rf_guard = {}
        inst = 0
        self._pend1 = None
        self._pend2 = None
        for hp in range(NH // 2):
            for qq in range(4):
                par = inst % 2
                last_mm = {}
                first_mm = {0: True, 1: True}
                prev = None

                def emit_av(ctx, par=par, hp=hp, qq=qq, last_mm=last_mm,
                            first_mm=first_mm):
                    kt, tE, gidx = ctx
                    for h in range(2):
                        hh = 2 * hp + h
                        bi = 4 + 2 * par + h
                        nb = self.pnum(par, h)
                        block = sPT[gidx % 2][:, 512 * h: 512 * h + 512]
                        vs = 65 * (NH * kt + hh)
                        stop_f = (kt == 0)
                        fdeps = [bank_guard.get((par, h))] if first_mm[h] else []
                        if kt > 4 * qq + 3:
                            t1 = P.op("pe", lambda e, nb=nb, block=block,
                                      st_f=first_mm[h]: e.matmul(
                                nb, self.z65[:, :], block,
                                start=st_f, stop=False), after=[tE] + fdeps)
                            first_mm[h] = False
                            pt_readers[gidx % 2].append(t1)
                            last_mm[h] = t1
                        elif kt >= 4 * qq:
                            o = kt - 4 * qq
                            sl = (h, o % 2)
                            t3 = P.op("pe", lambda e, bi=bi, block=block, o=o,
                                      st_f=first_mm[h]: e.matmul(
                                self.ps[0:65, 512 * bi: 512 * bi + 128 * (o + 1)],
                                self.z65[:, :], block[:, 0: 128 * (o + 1)],
                                start=st_f, stop=False), after=[tE] + fdeps)
                            first_mm[h] = False
                            pt_readers[gidx % 2].append(t3)
                            if o < 3:
                                t2 = P.op("pe", lambda e, bi=bi, block=block,
                                          vs=vs, o=o: e.matmul(
                                    self.ps[0:65, 512 * bi + 128 * (o + 1):
                                            512 * bi + 512],
                                    sV[:, vs:vs + 65],
                                    block[:, 128 * (o + 1): 512],
                                    start=False, stop=False), after=[tE])
                                pt_readers[gidx % 2].append(t2)
                            tMk = P.op("dve", lambda e, h=h, o=o, block=block:
                                       e.tensor_tensor(
                                sPM2[2 * h + o % 2][:, :],
                                block[:, 128 * o: 128 * o + 128],
                                self.sMask[:, :], op=ALU.mult),
                                after=[tE, pm_guard.get(sl)])
                            pt_readers[gidx % 2].append(tMk)
                            t1 = P.op("pe", lambda e, bi=bi, h=h, o=o, vs=vs,
                                      stop_f=stop_f: e.matmul(
                                self.ps[0:64, 512 * bi + 128 * o:
                                        512 * bi + 128 * o + 128],
                                sV[:, vs:vs + 64],
                                sPM2[2 * h + o % 2][:, :],
                                start=False, stop=stop_f), after=[tMk])
                            pm_guard[sl] = t1
                            last_mm[h] = t1
                        else:
                            t1 = P.op("pe", lambda e, nb=nb, block=block, vs=vs,
                                      stop_f=stop_f: e.matmul(
                                nb, sV[:, vs:vs + 65], block,
                                start=False, stop=stop_f), after=[tE])
                            pt_readers[gidx % 2].append(t1)
                            last_mm[h] = t1

                for g in range(NTT):
                    kt = 15 - g
                    par2 = g % 2
                    tSs = []
                    for h in range(2):
                        pr = prs[h]
                        pSb = self.ps[:, 1024 * par2 + 512 * h:
                                      1024 * par2 + 512 * h + 512]
                        tS = P.op("pe", lambda e, hp=hp, h=h, kt=kt, qq=qq, pr=pr,
                                  pSb=pSb: e.matmul(
                            pSb,
                            sK[pr[0]:pr[1], T * hp + 128 * kt: T * hp + 128 * kt + 128],
                            sQ[pr[0]:pr[1], T * hp + 512 * qq: T * hp + 512 * qq + 512],
                            start=True, stop=True),
                            after=[exp_tok.get(par2)])
                        tSs.append(tS)
                    tE = P.op("act", lambda e, par2=par2: e.activation(
                        sPT[par2][:, :], self.ps[:, 1024 * par2: 1024 * par2 + 1024],
                        AF.Exp, scale=QSCALE),
                        after=tSs + pt_readers[par2])
                    pt_readers[par2] = []
                    exp_tok[par2] = tE
                    if prev is not None:
                        emit_av(prev)
                    if g == 1 and self._pend1 is not None:
                        self._pend1()
                        self._pend1 = None
                    if g == 3 and self._pend2 is not None:
                        self._pend2()
                        self._pend2 = None
                    prev = (kt, tE, g)
                emit_av(prev)

                def fin1(par=par, hp=hp, qq=qq, last_mm=last_mm):
                    toks = {}
                    for h in range(2):
                        bi = 4 + 2 * par + h
                        tLn = P.op("act", lambda e, bi=bi: e.activation(
                            self.ps[96:97, 512 * bi: 512 * bi + 512],
                            self.ps[64:65, 512 * bi: 512 * bi + 512], AF.Ln),
                            after=[last_mm[h], rf_guard.get((par, h))])
                        tR = P.op("act", lambda e, h=h, bi=bi: e.activation(
                            rden_sb[h][:, :],
                            self.ps[96:97, 512 * bi: 512 * bi + 512],
                            AF.Exp, scale=-1.0),
                            after=[tLn, rden_guard.get(h)])
                        rf_guard[(par, h)] = tR
                        tEv = P.op("dve", lambda e, h=h, bi=bi: e.tensor_copy(
                            num_sb[h],
                            self.ps[0:64, 512 * bi: 512 * bi + 512]),
                            after=[last_mm[h], num_guard.get(h)])
                        toks[h] = (tR, tEv)

                    def fin2():
                        for h in range(2):
                            bi = 4 + 2 * par + h
                            tR, tEv = toks[h]
                            tB = P.op("pe", lambda e, h=h, bi=bi: e.matmul(
                                self.ps[0:64, 512 * bi: 512 * bi + 512],
                                self.ones128[0:1, 0:64], rden_sb[h][:, :],
                                start=True, stop=True), after=[tR, tEv])
                            rden_guard[h] = tB
                            tM = P.op("dve", lambda e, h=h, bi=bi:
                                      e.tensor_tensor(
                                num_sb[h], num_sb[h],
                                self.ps[0:64, 512 * bi: 512 * bi + 512],
                                op=ALU.mult), after=[tB])
                            bank_guard[(par, h)] = tM
                            dst = self.x_bf[64 * h:64 * h + 64,
                                            T * hp + 512 * qq: T * hp + 512 * qq + 512]
                            tX = P.op("dve", lambda e, h=h, dst=dst:
                                      e.tensor_tensor(
                                dst, dst, num_sb[h],
                                op=ALU.add), after=[tM])
                            num_guard[h] = tX
                    self._pend2 = fin2
                self._pend1 = fin1
                inst += 1
        if self._pend1 is not None:
            self._pend1()
            self._pend1 = None
        if self._pend2 is not None:
            self._pend2()
            self._pend2 = None
        P.barrier()


    def _proj_qkv_l0(self, bufA, bufB, sQ, sK, sV):
        P = self.P

        def evac_q(dst):
            def f(co, tq, pb, tok):
                return P.op("dve", lambda e, co=co, tq=tq, pb=pb, dst=dst: e.tensor_copy(
                    dst[:, T * co + 512 * tq: T * co + 512 * tq + 512], pb[:, :]), after=[tok])
            return f
        pbg = {}
        tldA = [P.dma("sp", lambda e: e.dma_start(bufA[:, 0:4096], self.dWq0[:, 0:4096])),
                P.dma("pool", lambda e: e.dma_start(bufA[:, 4096:8192], self.dWq0[:, 4096:8192]))]
        tldB = [P.dma("sp", lambda e: e.dma_start(bufB[:, 0:4096], self.dWk0[:, 0:4096])),
                P.dma("pool", lambda e: e.dma_start(bufB[:, 4096:8192], self.dWk0[:, 4096:8192]))]
        tq_use = self.emit_proj_w(bufA, evac_q(sQ), tldA, pbg)
        tldV = [P.dma("sp", lambda e: e.dma_start(bufA[:, 0:4096], self.dWv0[:, 0:4096]), after=[tq_use]),
                P.dma("pool", lambda e: e.dma_start(bufA[:, 4096:8192], self.dWv0[:, 4096:8192]), after=[tq_use])]
        self.emit_proj_w(bufB, evac_q(sK), tldB, pbg)
        tprev = pbg
        tlast = None
        for tt in range(NTT):
            for nt in range(2):
                bank = (tt * 2 + nt) % 2
                pb = self.pbank(bank)
                tmm = None
                for ci in range(NCI):
                    tmm = P.op("pe", lambda e, ci=ci, tt=tt, nt=nt, pb=pb: e.matmul(
                        pb[:, :],
                        self.x_bf[:, T * ci + 128 * tt: T * ci + 128 * tt + 128],
                        bufA[:, D * ci + 512 * nt: D * ci + 512 * nt + 512],
                        start=(ci == 0), stop=(ci == NCI - 1)),
                        after=[tprev.get(bank)] +
                              (tldV if tt == 0 and nt == 0 else []))
                vbase = 65 * (NH * tt + 8 * nt)
                te = P.op("dve", lambda e, pb=pb, vbase=vbase: e.tensor_copy(
                    sV.ap()[:, vbase: vbase + 520].rearrange(
                        "p (n c) -> p n c", c=65)[:, :, 0:64],
                    pb.rearrange("p (n c) -> p n c", c=64)), after=[tmm])
                tprev[bank] = te
                tlast = te
        return tlast

    def emit_proj_w(self, wbuf, evac, wait_tok, pbg, src=None):
        P = self.P
        if src is None:
            src = self.x_bf
        if not isinstance(wait_tok, list):
            wait_tok = [wait_tok]
        tmm = None
        for co in range(NCI):
            for tq in range(4):
                bank = (co * 4 + tq) % 2
                pb = self.pbank(bank)
                tmm = None
                for ci in range(NCI):
                    tmm = P.op("pe", lambda e, ci=ci, co=co, tq=tq, pb=pb, src=src: e.matmul(
                        pb[:, :],
                        wbuf[:, D * ci + 128 * co: D * ci + 128 * co + 128],
                        src[:, T * ci + 512 * tq: T * ci + 512 * tq + 512],
                        start=(ci == 0), stop=(ci == NCI - 1)),
                        after=[pbg.get(bank)] +
                              (wait_tok if co == 0 and tq == 0 and ci == 0 else []))
                pbg[bank] = evac(co, tq, pb, tmm)
        return tmm


    def l0_ffn(self, b, skip_ln=False):
        P, ar = self.P, self.ar
        ar.reset()
        h_bf = ar.alloc("h_bf", [128, NHC * T], BF16)
        self._h_bf = h_bf
        wgc = [ar.alloc(f"wgc{i}", [128, NCI * 128], BF16) for i in range(2)]
        wuc = [ar.alloc(f"wuc{i}", [128, NCI * 128], BF16) for i in range(2)]
        wdc = [ar.alloc(f"wdc{i}", [128, NHC * 128], BF16) for i in range(2)]
        sSil = [ar.alloc(f"sSil{i}", [128, 1024], BF16) for i in range(2)]

        g_use = {}
        sil_use = {}
        bank_use = {}
        hdone = {}
        for hc in range(NHC):
            wb = hc % 2
            tlg = P.dma("sp", lambda e, hc=hc, wb=wb: e.dma_start(
                wgc[wb][:, :], self.dWgT[:, 1024 * hc:1024 * hc + 1024]), after=[g_use.get(wb)])
            tlu = P.dma("pool", lambda e, hc=hc, wb=wb: e.dma_start(
                wuc[wb][:, :], self.dWuT[:, 1024 * hc:1024 * hc + 1024]), after=[g_use.get(wb)])
            for th in range(2):
                pp = (hc * 2 + th) % 2
                pG = self.ps[:, 1024 * pp:1024 * pp + 1024]
                pU = self.ps[:, 2048 + 1024 * pp:2048 + 1024 * pp + 1024]
                tg = tu = None
                for sub in range(2):
                    for ci in range(NCI):
                        tg = P.op("pe", lambda e, ci=ci, th=th, sub=sub, wb=wb, pG=pG: e.matmul(
                            pG[:, 512 * sub:512 * sub + 512],
                            wgc[wb][:, 128 * ci:128 * ci + 128],
                            self.x_bf[:, T * ci + 1024 * th + 512 * sub:
                                      T * ci + 1024 * th + 512 * sub + 512],
                            start=(ci == 0), stop=(ci == NCI - 1)),
                            after=[tlg, bank_use.get(("G", pp))])
                for sub in range(2):
                    for ci in range(NCI):
                        tu = P.op("pe", lambda e, ci=ci, th=th, sub=sub, wb=wb, pU=pU: e.matmul(
                            pU[:, 512 * sub:512 * sub + 512],
                            wuc[wb][:, 128 * ci:128 * ci + 128],
                            self.x_bf[:, T * ci + 1024 * th + 512 * sub:
                                      T * ci + 1024 * th + 512 * sub + 512],
                            start=(ci == 0), stop=(ci == NCI - 1)),
                            after=[tlu, bank_use.get(("U", pp))])
                g_use[wb] = tu
                tsil = P.op("act", lambda e, pp=pp, pG=pG: e.activation(
                    sSil[pp][:, :], pG[:, :], AF.Silu), after=[tg, sil_use.get(pp)])
                thm = P.op("dve", lambda e, hc=hc, th=th, pp=pp, pU=pU: e.tensor_tensor(
                    h_bf[:, T * hc + 1024 * th: T * hc + 1024 * th + 1024],
                    sSil[pp][:, :], pU[:, :], op=ALU.mult), after=[tu, tsil])
                sil_use[pp] = thm
                bank_use[("G", pp)] = tsil
                bank_use[("U", pp)] = thm
                hdone[(hc, th)] = thm
        d_use = {}
        dbank_use = {}
        for co in range(NCI):
            wb = co % 2
            tld = P.dma(("sp", "pool")[co % 2], lambda e, co=co, wb=wb: e.dma_start(
                wdc[wb][:, :], self.dWdT[:, 3072 * co:3072 * co + 3072]), after=[d_use.get(wb)])
            tmm = None
            for tq in range(4):
                pp = (co * 4 + tq) % 2
                pD = self.pbank(pp)
                deps = [tld, dbank_use.get(pp)] + [hdone[(hc, tq // 2)] for hc in range(NHC)]
                tmm = None
                for hc in range(NHC):
                    tmm = P.op("pe", lambda e, hc=hc, co=co, tq=tq, wb=wb, pD=pD: e.matmul(
                        pD[:, :],
                        wdc[wb][:, 128 * hc:128 * hc + 128],
                        h_bf[:, T * hc + 512 * tq: T * hc + 512 * tq + 512],
                        start=(hc == 0), stop=(hc == NHC - 1)),
                        after=deps if hc == 0 else [])
                tev = P.op("dve", lambda e, co=co, tq=tq, pD=pD: e.tensor_tensor(
                    self.x_bf[:, T * co + 512 * tq: T * co + 512 * tq + 512],
                    self.x_bf[:, T * co + 512 * tq: T * co + 512 * tq + 512],
                    pD[:, :], op=ALU.add), after=[tmm])
                dbank_use[pp] = tev
            d_use[wb] = tmm
        P.barrier()
        ar.reset()
        if not skip_ln:
            self._ln(b)

    def l1_attn(self, b):
        P, ar = self.P, self.ar
        ar.reset()
        sQ = ar.alloc("sQ1", [128, NCI * T], BF16)
        sAO = ar.alloc("sAO", [128, NCI * T], BF16)
        kv2 = ar.alloc("kv2", [128, T], BF16)
        sKVtok = ar.alloc("sKVtok", [128, NTT * 65], BF16)
        bufA = ar.alloc("wbufA1", [128, NCI * D], BF16)
        bufB = ar.alloc("wbufB1", [128, NCI * D], BF16)
        sPT = [ar.alloc(f"sPT1{p}", [128, 1024], BF16) for p in range(2)]
        sPM2 = [ar.alloc(f"sPM1m{i}", [128, 128], BF16) for i in range(4)]
        num_sb = [[ar.alloc(f"nsb1{p}{h}", [64, 512], F32) for h in range(2)]
                  for p in range(2)]
        rden_f32 = [[ar.alloc(f"rf1{p}{h}", [1, 512], F32) for h in range(2)]
                    for p in range(2)]
        rden_sb = [[ar.alloc(f"rd1{p}{h}", [1, 512], BF16) for h in range(2)]
                   for p in range(2)]
        sPnAll = [ar.alloc(f"sPnA{h}", [1, T], BF16) for h in range(2)]

        pbg = {}
        tldA = [P.dma("sp", lambda e: e.dma_start(bufA[:, 0:4096], self.dWq1[:, 0:4096])),
                P.dma("pool", lambda e: e.dma_start(bufA[:, 4096:8192], self.dWq1[:, 4096:8192]))]
        tldB = [P.dma("sp", lambda e: e.dma_start(bufB[:, 0:4096], self.dWout[:, 0:4096])),
                P.dma("pool", lambda e: e.dma_start(bufB[:, 4096:8192], self.dWout[:, 4096:8192]))]

        def evac_q(co, tq, pb, tok):
            return P.op("dve", lambda e, co=co, tq=tq, pb=pb: e.tensor_copy(
                sQ[:, T * co + 512 * tq: T * co + 512 * tq + 512], pb[:, :]), after=[tok])
        self.emit_proj_w(bufA, evac_q, tldA, pbg)

        kv_evac = []
        for tq in range(4):
            bank = tq % 2
            pb = self.pbank(bank)
            tmm = None
            for ci in range(NCI):
                tmm = P.op("pe", lambda e, ci=ci, tq=tq, pb=pb: e.matmul(
                    pb[0:64, :], self.sWkv[:, HD * ci:HD * ci + HD],
                    self.x_bf[:, T * ci + 512 * tq: T * ci + 512 * tq + 512],
                    start=(ci == 0), stop=(ci == NCI - 1)),
                    after=[pbg.get(bank)] if ci == 0 else [])
            te = P.op("dve", lambda e, tq=tq, pb=pb: e.tensor_copy(
                kv2[0:64, 512 * tq:512 * tq + 512], pb[0:64, :]), after=[tmm])
            pbg[bank] = te
            kv_evac.append(te)
        tdup = P.dma("pool", lambda e: e.dma_start(kv2[64:128, :], kv2[0:64, :]),
                     after=kv_evac)
        for tt in range(NTT):
            bank = tt % 2
            pb = self.pbank(bank)
            tmm = None
            for ci in range(NCI):
                tmm = P.op("pe", lambda e, ci=ci, tt=tt, pb=pb: e.matmul(
                    pb[:, 0:64],
                    self.x_bf[:, T * ci + 128 * tt: T * ci + 128 * tt + 128],
                    self.sWkv[:, HD * ci:HD * ci + HD],
                    start=(ci == 0), stop=(ci == NCI - 1)),
                    after=[pbg.get(bank)] if ci == 0 else [])
            te = P.op("dve", lambda e, tt=tt, pb=pb: e.tensor_copy(
                sKVtok[:, 65 * tt:65 * tt + 64], pb[:, 0:64]), after=[tmm])
            pbg[bank] = te
        tones = P.op("pool", lambda e: e.memset(
            sKVtok.ap().rearrange("p (n c) -> p n c", c=65)[:, :, 64:65], 1.0),
            after=[P.last("dve")])
        P.barrier()

        prs = [(0, 64), (64, 128)]
        pt_readers = {p: [] for p in range(2)}
        exp_tok = {}
        null_read = {0: [], 1: []}
        pm_guard = {}
        bank_guard = {}
        num_guard = {}
        rden_guard = {}
        rf_guard = {}
        inst = 0
        self._pend1 = None
        self._pend2 = None
        for hp in range(NH // 2):
            tEn = {}
            all_exp = [exp_tok.get(p) for p in range(2)]
            for h in range(2):
                pr = prs[h]
                pnq = []
                for q4 in range(4):
                    pnq.append(P.op("pe", lambda e, hp=hp, h=h, q4=q4, pr=pr: e.matmul(
                        self.ps[0:1, 512 * q4: 512 * q4 + 512],
                        self.nk_col[pr[0]:pr[1], 0:1],
                        sQ[pr[0]:pr[1], T * hp + 512 * q4: T * hp + 512 * q4 + 512],
                        start=True, stop=True),
                        after=all_exp + [tEn.get(0)]))
                tEn[h] = P.op("act", lambda e, h=h: e.activation(
                    sPnAll[h][:, :], self.ps[0:1, 0:2048],
                    AF.Exp, scale=QSCALE), after=pnq + null_read[h])
                null_read[h] = []
            for p in range(2):
                exp_tok[p] = tEn[1]
            for qq in range(4):
                par = inst % 2
                last_mm = {}
                for h in range(2):
                    nb = self.pnum(par, h)
                    tN = P.op("pe", lambda e, nb=nb, h=h, qq=qq: e.matmul(
                        nb, self.nkv65[0:1, :],
                        sPnAll[h][0:1, 512 * qq: 512 * qq + 512],
                        start=True, stop=False),
                        after=[tEn[h], bank_guard.get((par, h))])
                    null_read[h].append(tN)
                prev = None

                def emit_av(ctx, par=par, hp=hp, qq=qq, last_mm=last_mm):
                    kt, tE, gidx = ctx
                    for h in range(2):
                        bi = 4 + 2 * par + h
                        nb = self.pnum(par, h)
                        block = sPT[gidx % 2][:, 512 * h: 512 * h + 512]
                        vs = 65 * kt
                        stop_f = (kt == 0)
                        if kt >= 4 * qq:
                            o = kt - 4 * qq
                            sl = (h, o % 2)
                            if o < 3:
                                t2 = P.op("pe", lambda e, bi=bi, block=block,
                                          vs=vs, o=o: e.matmul(
                                    self.ps[0:65, 512 * bi + 128 * (o + 1):
                                            512 * bi + 512],
                                    sKVtok[:, vs:vs + 65],
                                    block[:, 128 * (o + 1): 512],
                                    start=False, stop=False), after=[tE])
                                pt_readers[gidx % 2].append(t2)
                            tMk = P.op("dve", lambda e, h=h, o=o, block=block:
                                       e.tensor_tensor(
                                sPM2[2 * h + o % 2][:, :],
                                block[:, 128 * o: 128 * o + 128],
                                self.sMask[:, :], op=ALU.mult),
                                after=[tE, pm_guard.get(sl)])
                            pt_readers[gidx % 2].append(tMk)
                            t1 = P.op("pe", lambda e, bi=bi, h=h, o=o, vs=vs,
                                      stop_f=stop_f: e.matmul(
                                self.ps[0:65, 512 * bi + 128 * o:
                                        512 * bi + 128 * o + 128],
                                sKVtok[:, vs:vs + 65],
                                sPM2[2 * h + o % 2][:, :],
                                start=False, stop=stop_f), after=[tMk])
                            pm_guard[sl] = t1
                        else:
                            t1 = P.op("pe", lambda e, nb=nb, block=block, vs=vs,
                                      stop_f=stop_f: e.matmul(
                                nb, sKVtok[:, vs:vs + 65], block,
                                start=False, stop=stop_f), after=[tE])
                            pt_readers[gidx % 2].append(t1)
                        last_mm[h] = t1

                nkt = 4 * qq + 4
                for g in range(nkt):
                    kt = nkt - 1 - g
                    par2 = g % 2
                    tSs = []
                    for h in range(2):
                        pr = prs[h]
                        pSb = self.ps[:, 1024 * par2 + 512 * h:
                                      1024 * par2 + 512 * h + 512]
                        tS = P.op("pe", lambda e, h=h, kt=kt, qq=qq, hp=hp, pr=pr,
                                  pSb=pSb: e.matmul(
                            pSb,
                            kv2[pr[0]:pr[1], 128 * kt: 128 * kt + 128],
                            sQ[pr[0]:pr[1], T * hp + 512 * qq: T * hp + 512 * qq + 512],
                            start=True, stop=True),
                            after=[exp_tok.get(par2)])
                        tSs.append(tS)
                    tE = P.op("act", lambda e, par2=par2: e.activation(
                        sPT[par2][:, :], self.ps[:, 1024 * par2: 1024 * par2 + 1024],
                        AF.Exp, scale=QSCALE),
                        after=tSs + pt_readers[par2])
                    pt_readers[par2] = []
                    exp_tok[par2] = tE
                    if prev is not None:
                        emit_av(prev)
                    if g == 1 and self._pend1 is not None:
                        self._pend1()
                        self._pend1 = None
                    if g == 3 and self._pend2 is not None:
                        self._pend2()
                        self._pend2 = None
                    prev = (kt, tE, g)
                emit_av(prev)

                def fin1(par=par, hp=hp, qq=qq, last_mm=last_mm):
                    toks = {}
                    for h in range(2):
                        bi = 4 + 2 * par + h
                        tLn = P.op("act", lambda e, par=par, h=h, bi=bi:
                                   e.activation(
                            rden_f32[par][h][:, :],
                            self.ps[64:65, 512 * bi: 512 * bi + 512], AF.Ln),
                            after=[last_mm[h], rf_guard.get((par, h))])
                        tR = P.op("act", lambda e, par=par, h=h: e.activation(
                            rden_sb[par][h][:, :], rden_f32[par][h][:, :],
                            AF.Exp, scale=-1.0),
                            after=[tLn, rden_guard.get((par, h))])
                        rf_guard[(par, h)] = tR
                        tEv = P.op("dve", lambda e, par=par, h=h, bi=bi: e.tensor_copy(
                            num_sb[par][h][:, :],
                            self.ps[0:64, 512 * bi: 512 * bi + 512]),
                            after=[last_mm[h], num_guard.get((par, h))])
                        toks[h] = (tR, tEv)

                    def fin2():
                        for h in range(2):
                            bi = 4 + 2 * par + h
                            tR, tEv = toks[h]
                            tB = P.op("pe", lambda e, par=par, h=h, bi=bi: e.matmul(
                                self.ps[0:64, 512 * bi: 512 * bi + 512],
                                self.ones128[0:1, 0:64], rden_sb[par][h][:, :],
                                start=True, stop=True), after=[tR, tEv])
                            rden_guard[(par, h)] = tB
                            dst = sAO[64 * h:64 * h + 64,
                                      T * hp + 512 * qq: T * hp + 512 * qq + 512]
                            tM = P.op("dve", lambda e, par=par, h=h, bi=bi, dst=dst:
                                      e.tensor_tensor(
                                dst, num_sb[par][h][:, :],
                                self.ps[0:64, 512 * bi: 512 * bi + 512],
                                op=ALU.mult), after=[tB])
                            bank_guard[(par, h)] = tM
                            num_guard[(par, h)] = tM
                    self._pend2 = fin2
                self._pend1 = fin1
                inst += 1
        if self._pend1 is not None:
            self._pend1()
            self._pend1 = None
        if self._pend2 is not None:
            self._pend2()
            self._pend2 = None
        P.barrier()


        pbg = {}

        def evac_out(co, tq, pb, tok):
            return P.op("dve", lambda e, co=co, tq=tq, pb=pb: e.tensor_tensor(
                self.x_bf[:, T * co + 512 * tq: T * co + 512 * tq + 512],
                self.x_bf[:, T * co + 512 * tq: T * co + 512 * tq + 512],
                pb[:, :], op=ALU.add), after=[tok])
        self.emit_proj_w(bufB, evac_out, tldB, pbg, src=sAO)
        P.barrier()
        ar.reset()
        self._ln(b)

    def l1_ffn_final(self, b, next_b=None):
        P, ar = self.P, self.ar
        ar.reset()
        xNew = ar.alloc("xNew", [128, NCI * T], BF16)
        pbg = {}
        for co in range(NCI):
            for tq in range(4):
                bank = (co * 4 + tq) % 2
                pb = self.pbank(bank)
                tmm = None
                for ci in range(NCI):
                    tmm = P.op("pe", lambda e, ci=ci, co=co, tq=tq, pb=pb: e.matmul(
                        pb[:, :],
                        self.M_bf[:, D * ci + 128 * co: D * ci + 128 * co + 128],
                        self.x_bf[:, T * ci + 512 * tq: T * ci + 512 * tq + 512],
                        start=(ci == 0), stop=(ci == NCI - 1)),
                        after=[pbg.get(bank)] if ci == 0 else [])
                tev = P.op("dve", lambda e, co=co, tq=tq, pb=pb: e.scalar_tensor_tensor(
                    xNew[:, T * co + 512 * tq: T * co + 512 * tq + 512],
                    pb[:, :], self.sBias[:, co:co + 1],
                    self.x_bf[:, T * co + 512 * tq: T * co + 512 * tq + 512],
                    op0=ALU.add, op1=ALU.add), after=[tmm])
                pbg[bank] = tev
        P.barrier()
        self._ln(b, buf=xNew)
        if next_b is not None:
            self.load_x(next_b, barrier=False)
        yout = [ar.alloc(f"fyout{i}", [128, D], F32) for i in range(2)]
        ptr = [self.ps[:, 512 * i:512 * i + 64].bitcast(BF16) for i in range(2)]
        tr_use = {}
        ys_use = {}
        for tt in range(NTT):
            pi = tt % 2
            tevs = []
            for ci in range(NCI):
                tpi = (tt * NCI + ci) % 2
                t1 = P.op("pe", lambda e, ci=ci, tt=tt, tpi=tpi: e.transpose(
                    ptr[tpi][:, 0:128],
                    xNew[:, T * ci + 128 * tt: T * ci + 128 * tt + 128],
                    self.ident[:, :]), after=[tr_use.get(tpi)])
                te = P.op("dve", lambda e, ci=ci, pi=pi, tpi=tpi: e.tensor_copy(
                    yout[pi][:, 128 * ci:128 * ci + 128], ptr[tpi][:, 0:128]),
                    after=[t1, ys_use.get(pi) if ci == 0 else None])
                tr_use[tpi] = te
                tevs.append(te)
            ys_use[pi] = P.dma(("sp", "pool")[pi], lambda e, b=b, tt=tt, pi=pi:
                               e.dma_start(
                self.y_out[b, 128 * tt:128 * tt + 128, :], yout[pi][:, :]), after=tevs)
        P.barrier()

    def _ln(self, b, buf=None):
        P, ar = self.P, self.ar
        if buf is None:
            buf = self.x_bf
        off0 = ar.mark()
        sXsq = [ar.alloc(f"lnXsq{i}", [128, T], BF16) for i in range(2)]
        sM2 = ar.alloc("lnM2", [128, T], BF16)
        sVar = ar.alloc("lnVar", [128, T], BF16)
        sLn = ar.alloc("lnLn", [128, T], F32)
        sRstd = ar.alloc("lnRstd", [128, T], BF16)
        sMean = ar.alloc("lnMean", [128, T], BF16)
        sXm2 = ar.alloc("lnXm2", [128, T], BF16)
        sXm = ar.alloc("lnXm", [128, T], BF16)
        pA = self.ps[:, 0:2048]
        pB = self.ps[:, 2048:4096]
        tsq = {}
        tmmA = None
        tmmB = None
        for ci in range(NCI):
            pi = ci % 2
            tq_ = P.op("act", lambda e, ci=ci, pi=pi: e.activation(
                sXsq[pi][:, :], buf[:, T * ci:T * ci + T], AF.Square),
                after=[tsq.get(pi)])
            for t4 in range(4):
                tmmA = P.op("pe", lambda e, ci=ci, t4=t4: e.matmul(
                    pA[:, 512 * t4:512 * t4 + 512], self.ones128[:, :],
                    buf[:, T * ci + 512 * t4: T * ci + 512 * t4 + 512],
                    start=(ci == 0), stop=(ci == NCI - 1)), after=[])
                tmmB = P.op("pe", lambda e, ci=ci, t4=t4, pi=pi: e.matmul(
                    pB[:, 512 * t4:512 * t4 + 512], self.ones128[:, :],
                    sXsq[pi][:, 512 * t4:512 * t4 + 512],
                    start=(ci == 0), stop=(ci == NCI - 1)), after=[tq_])
            tsq[pi] = tmmB
        tm2 = P.op("act", lambda e: e.activation(sM2[:, :], pA[:, :], AF.Square,
                                                 scale=1.0 / D), after=[tmmA])
        tMean = P.op("act", lambda e: e.activation(sMean[:, :], pA[:, :], AF.Copy,
                                                   scale=1.0 / D), after=[tmmA])
        tv = P.op("dve", lambda e: e.scalar_tensor_tensor(
            sVar[:, :], pB[:, :], 1.0 / D, sM2[:, :], op0=ALU.mult, op1=ALU.subtract),
            after=[tmmB, tm2])
        tl_ = P.op("act", lambda e: e.activation(sLn[:, :], sVar[:, :], AF.Ln, bias=EPS),
                   after=[tv])
        tr_ = P.op("act", lambda e: e.activation(sRstd[:, :], sLn[:, :], AF.Exp, scale=-0.5),
                   after=[tl_])
        lastap = {}
        for ci in range(NCI):
            eng = "dve"
            sx = sXm if ci % 2 == 0 else sXm2
            txm = P.op(eng, lambda e, ci=ci, sx=sx: e.tensor_tensor(
                sx[:, :], buf[:, T * ci:T * ci + T], sMean[:, :],
                op=ALU.subtract), after=[tMean, lastap.get(eng)])
            lastap[eng] = P.op(eng, lambda e, ci=ci, sx=sx: e.tensor_tensor(
                buf[:, T * ci:T * ci + T], sx[:, :], sRstd[:, :], op=ALU.mult),
                after=[tr_, txm])
        P.barrier()
        ar.reset(off0)


from concourse.bass_utils import run_bass_kernel_spmd

N_CORES = 8
B_FULL = 32
B_CORE = B_FULL // N_CORES


def _host_consts():
    nk = np.arange(D)
    ang = 2.0 * np.pi * np.outer(nk, nk) / D
    Ctab = np.cos(ang).astype(np.float32)
    Stab = np.sin(ang).astype(np.float32)
    masks = (np.arange(128)[:, None] <= np.arange(128)[None, :]).astype(np.float32)
    ident = np.eye(128, dtype=np.float32)
    return Ctab, Stab, masks, ident


def _build():
    k = K(n_batch=B_CORE)
    k.prep_attn_weights()
    k.prep_ffn_weights(barrier=False)
    k.prep_M(barrier=False, ar_off=28672)
    k.load_x(0, barrier=False)
    k.P.barrier()
    for b in range(B_CORE):
        k.l0_attn(b)
        k.l0_ffn(b)
        k.l1_attn(b)
        k.l1_ffn_final(b, next_b=b + 1 if b + 1 < B_CORE else None)
    k.nc.finalize()
    return k


def kernel(L0_attn_content, L0_attn_Wq, L0_attn_Wk, L0_attn_Wv,
           L0_ffn_w_gate, L0_ffn_w_up, L0_ffn_w_down,
           L1_attn_Wq, L1_attn_Wkv, L1_attn_null_kv, L1_attn_Wout,
           L1_ffn_w_freq_real, L1_ffn_w_freq_imag, L1_ffn_bias):
    Ctab, Stab, masks, ident = _host_consts()
    x = np.ascontiguousarray(np.asarray(L0_attn_content, dtype=np.float32))
    shared = {
        "Wq0": np.asarray(L0_attn_Wq, np.float32),
        "Wk0": np.asarray(L0_attn_Wk, np.float32),
        "Wv0": np.asarray(L0_attn_Wv, np.float32),
        "Wg": np.asarray(L0_ffn_w_gate, np.int32),
        "Wu": np.asarray(L0_ffn_w_up, np.int32),
        "Wd": np.asarray(L0_ffn_w_down, np.int32),
        "Wq1": np.asarray(L1_attn_Wq, np.float32),
        "Wkv": np.asarray(L1_attn_Wkv, np.float32),
        "nullkv": np.asarray(L1_attn_null_kv, np.float32),
        "Wout": np.asarray(L1_attn_Wout, np.float32),
        "wfr": np.asarray(L1_ffn_w_freq_real, np.float32),
        "wfi": np.asarray(L1_ffn_w_freq_imag, np.float32),
        "bias1": np.asarray(L1_ffn_bias, np.float32),
        "Ctab": Ctab, "Stab": Stab, "masks_in": masks, "ident_in": ident,
    }
    k = _build()
    in_maps = []
    for c in range(N_CORES):
        m = dict(shared)
        m["x_in"] = x[B_CORE * c: B_CORE * (c + 1)]
        in_maps.append(m)
    res = run_bass_kernel_spmd(k.nc, in_maps, core_ids=list(range(N_CORES)))
    return np.concatenate([np.asarray(r["y_out"]) for r in res.results], axis=0)


# version 26
# speedup vs baseline: 1.0168x; 1.0168x over previous
import numpy as np
import concourse.bass as bass
import concourse.mybir as mybir

F32 = mybir.dt.float32
BF16 = mybir.dt.bfloat16
I32 = mybir.dt.int32
U8 = mybir.dt.uint8
AF = mybir.ActivationFunctionType
ALU = mybir.AluOpType
AX = mybir.AxisListType

ENGINES = ("pe", "act", "dve", "pool", "sp")

D = 1024
T = 2048
NCI = 8
NTT = 16
NH = 16
HD = 64
HID = 3072
NHC = 24
EPS = 1e-5
QSCALE = 0.125
DEN_DVE = False


class Prog:
    NDMASEM = 8

    def __init__(self, nc):
        self.nc = nc
        self.sem = {e: nc.alloc_semaphore(f"sem_{e}") for e in ENGINES}
        self.cnt = {e: 0 for e in ENGINES}
        self.dma_i = {}
        for q in ("sp", "pool"):
            for r in range(self.NDMASEM):
                key = f"{q}dma{r}"
                self.sem[key] = nc.alloc_semaphore(f"sem_{key}")
                self.cnt[key] = 0
            self.dma_i[q] = 0
        self.ops = {e: [] for e in ENGINES}
        self.waited = {e: {p: 0 for p in self.sem} for e in ENGINES}
        self.n_ops = 0

    def op(self, eng, fn, after=(), inc=True, dma=False):
        waits = []
        if dma:
            q = eng
            slot = self.dma_i[q] % self.NDMASEM
            self.dma_i[q] += 1
            semkey = f"{q}dma{slot}"
            if self.cnt[semkey] > self.waited[eng][semkey]:
                waits.append((self.sem[semkey], self.cnt[semkey]))
                self.waited[eng][semkey] = self.cnt[semkey]
        else:
            semkey = eng
        for tok in after:
            if tok is None:
                continue
            pkey, pval = tok
            if (not dma) and pkey == eng == "pe":
                continue
            if pval > self.waited[eng][pkey]:
                waits.append((self.sem[pkey], pval))
                self.waited[eng][pkey] = pval
        amt = 16 if dma else 1
        tok = None
        if inc:
            self.cnt[semkey] += amt
            tok = (semkey, self.cnt[semkey])
        self.ops[eng].append((waits, fn, (self.sem[semkey], amt) if inc else None))
        self.n_ops += 1
        return tok

    def dma(self, eng, fn, after=()):
        return self.op(eng, fn, after=after, inc=True, dma=True)

    def last(self, eng):
        return (eng, self.cnt[eng])

    def barrier(self):
        nc = self.nc
        if not hasattr(self, "_bar_seen"):
            self._bar_seen = {k: 0 for k in self.cnt}
        for q in ("sp", "pool"):
            waits = []
            for r in range(self.NDMASEM):
                key = f"{q}dma{r}"
                if self.cnt[key] > self._bar_seen[key]:
                    waits.append((self.sem[key], self.cnt[key]))
                self._bar_seen[key] = self.cnt[key]
            if waits:
                self.ops[q].append((waits, None, None))
        with nc.Block() as blk:
            secmap = {"pe": blk.tensor, "act": blk.scalar, "dve": blk.vector,
                      "pool": blk.gpsimd, "sp": blk.sync}
            for e in ENGINES:
                eng_ops = self.ops[e]
                if not eng_ops:
                    continue

                def make(eng_ops=eng_ops):
                    def _(eng):
                        for waits, fn, inc in eng_ops:
                            for (sem, val) in waits:
                                eng.wait_ge(sem, val)
                            if fn is None:
                                continue
                            inst = fn(eng)
                            if inc is not None:
                                inst.then_inc(inc[0], inc[1])
                    return _
                secmap[e](make())
        self.ops = {e: [] for e in ENGINES}
        for e in ENGINES:
            for p in self.sem:
                self.waited[e][p] = self.cnt[p]


class Arena:
    def __init__(self, nc, name, nbytes, align=64):
        self.nc = nc
        self.h = nc.alloc_sbuf_tensor(name, [128, nbytes], U8)
        self.base = nc.lookup_mloc(self.h).addr
        self.off = 0
        self.nbytes = nbytes
        self.align = align
        self.n = 0

    def alloc(self, name, shape, dtype):
        isz = {F32: 4, BF16: 2, I32: 4, U8: 1}[dtype]
        per_part = int(np.prod(shape[1:])) * isz if len(shape) > 1 else isz
        a = self.align
        self.off = (self.off + a - 1) // a * a
        assert self.off + per_part <= self.nbytes, \
            f"arena overflow: {name} needs {per_part} at {self.off}/{self.nbytes}"
        h = self.nc.alloc_sbuf_tensor_at(f"{name}_{self.n}", list(shape), dtype,
                                         offset=self.base + self.off)
        self.n += 1
        self.off += per_part
        return h

    def mark(self):
        return self.off

    def reset(self, off=0):
        self.off = off


class K:

    def __init__(self, n_batch=4, dbg=None):
        self.nc = nc = bass.Bass("TRN2", target_bir_lowering=False)
        self.n_batch = n_batch
        self.dbg = dbg

        ein = lambda n, s, d=F32: nc.dram_tensor(n, s, d, kind="ExternalInput").ap()
        self.x_in = ein("x_in", [n_batch, T, D])
        self.Wq0 = ein("Wq0", [D, D]); self.Wk0 = ein("Wk0", [D, D]); self.Wv0 = ein("Wv0", [D, D])
        self.Wg = ein("Wg", [HID, D], I32); self.Wu = ein("Wu", [HID, D], I32)
        self.Wd = ein("Wd", [D, HID], I32)
        self.Wq1 = ein("Wq1", [D, D]); self.Wkv = ein("Wkv", [D, HD])
        self.nullkv = ein("nullkv", [HD]); self.Wout = ein("Wout", [D, D])
        self.wfr = ein("wfr", [D, D]); self.wfi = ein("wfi", [D, D])
        self.bias1 = ein("bias1", [D])
        self.Ctab = ein("Ctab", [D, D]); self.Stab = ein("Stab", [D, D])
        self.masks_in = ein("masks_in", [128, 128])
        self.ident_in = ein("ident_in", [128, 128])
        self.y_out = nc.dram_tensor("y_out", [n_batch, T, D], F32, kind="ExternalOutput").ap()
        if dbg is not None:
            self.x_dbg = nc.dram_tensor("x_dbg", [n_batch, 128, NCI * T], F32,
                                        kind="ExternalOutput").ap()

        dts = lambda n, s, d=BF16: nc.dram_tensor(n, s, d, kind="Internal").ap()
        self.dWq0 = dts("dWq0", [128, NCI * D]); self.dWk0 = dts("dWk0", [128, NCI * D])
        self.dWv0 = dts("dWv0", [128, NCI * D]); self.dWq1 = dts("dWq1", [128, NCI * D])
        self.dWout = dts("dWout", [128, NCI * D])
        self.dWgT = dts("dWgT", [128, NHC * NCI * 128])
        self.dWuT = dts("dWuT", [128, NHC * NCI * 128])
        self.dWdT = dts("dWdT", [128, NCI * NHC * 128])

        sb = lambda n, s, d: nc.alloc_sbuf_tensor(n, s, d)
        self.x_bf = sb("x_bf", [128, NCI * T], BF16)
        self.M_bf = sb("M_bf", [128, NCI * D], BF16)
        self.sMask = sb("sMask", [128, 128], BF16)
        self.z65 = sb("z65", [128, 65], BF16)
        self.ones128 = sb("ones128", [128, 128], BF16)
        self.onescol = sb("onescol", [128, 1], BF16)
        self.zrow = sb("zrow", [1, 260], BF16)
        self.ident = sb("ident", [128, 128], BF16)
        self.sWkv = sb("sWkv", [128, NCI * HD], BF16)
        self.nk_col = sb("nk_col", [128, 1], BF16)
        self.nkv65 = sb("nkv65", [1, 65], BF16)
        self.sBias = sb("sBias", [128, 8], F32)
        self.stage32 = sb("stage32", [128, 2048], F32)
        self.xtokp = sb("xtokp", [128, 2048], BF16)

        self.cEPS = sb("cEPS", [128, 1], F32)
        nc.const_aps.register(F32, EPS, self.cEPS[:, :])
        self.ps = nc.alloc_psum_tensor("ps", [128, 4096], F32)
        self.ar = Arena(nc, "arena", 144500)
        self.P = Prog(nc)
        self.bank_readers = {i: [] for i in range(8)}
        self.bank_writer = {i: None for i in range(8)}

    def pbank(self, i, n=1):
        return self.ps[:, 512 * i: 512 * (i + n)]

    def pnum(self, par, h, rows=65):
        bi = 4 + 2 * par + h
        return self.ps[0:rows, 512 * bi: 512 * bi + 512]

    def prep_attn_weights(self):
        P, nc = self.P, self.nc
        pairs = [(self.Wq0, self.dWq0), (self.Wk0, self.dWk0), (self.Wv0, self.dWv0),
                 (self.Wq1, self.dWq1), (self.Wout, self.dWout)]
        ar = self.ar
        ar.reset()
        NB = 6
        stg = [ar.alloc(f"wstg{i}", [128, 2048], F32) for i in range(NB)]
        sbf = [ar.alloc(f"wcast{i}", [128, 2048], BF16) for i in range(NB)]
        tcast = {}
        tstore = {}
        j = 0
        for (w, dw) in pairs:
            wr = w.rearrange("(c p) o -> p c o", p=128)
            for q in range(4):
                bi = j % NB
                qn = ("sp", "pool")[j % 2]
                tl = P.dma(qn, lambda e, wr=wr, q=q, bi=bi: e.dma_start(
                    stg[bi][:, :], wr[:, 2 * q:2 * q + 2, :]),
                    after=[tcast.get(bi)])
                tc = P.op("act", lambda e, bi=bi: e.copy(sbf[bi][:, :], stg[bi][:, :]),
                          after=[tl, tstore.get(bi)])
                tcast[bi] = tc
                ts = P.dma(("pool", "sp")[j % 2], lambda e, dw=dw, q=q, bi=bi:
                           e.dma_start(dw[:, 2048 * q:2048 * (q + 1)], sbf[bi][:, :]),
                           after=[tc])
                tstore[bi] = ts
                j += 1
        wr = self.Wkv.rearrange("(c p) o -> p c o", p=128)
        tl = P.dma("sp", lambda e: e.dma_start(self.stage32[:, 0:512], wr[:, :, :]))
        P.op("act", lambda e: e.copy(self.sWkv[:, :], self.stage32[:, 0:512]), after=[tl])
        nk = self.nullkv
        t1 = P.dma("pool", lambda e: e.dma_start(
            self.stage32[0:64, 1024:1025], nk.rearrange("(a b) -> a b", b=1)))
        t2 = P.dma("pool", lambda e: e.dma_start(
            self.stage32[64:128, 1024:1025], nk.rearrange("(a b) -> a b", b=1)), after=[t1])
        t3 = P.dma("pool", lambda e: e.dma_start(
            self.stage32[0:1, 1040:1104], nk.rearrange("(a b) -> a b", a=1)), after=[t2])
        tc1 = P.op("dve", lambda e: e.tensor_copy(self.nk_col[:, :], self.stage32[:, 1024:1025]), after=[t1, t2])
        tc2 = P.op("dve", lambda e: e.tensor_copy(self.nkv65[0:1, 0:64], self.stage32[0:1, 1040:1104]), after=[t3])
        P.op("pool", lambda e: e.memset(self.nkv65[0:1, 64:65], 1.0), inc=False)
        t4 = P.dma("pool", lambda e: e.dma_start(
            self.sBias[:, :], self.bias1.rearrange("(c p) -> p c", p=128),
            allow_slow_non_contiguous=True), after=[tc1, tc2])
        t5 = P.dma("sp", lambda e: e.dma_start(self.stage32[:, 0:128], self.masks_in[:, :]),
                   after=[tc1, tc2, P.last("act")])
        tMsk = P.op("dve", lambda e: e.tensor_copy(self.sMask[:, :], self.stage32[:, 0:128]),
                    after=[t5])
        tz = P.op("pool", lambda e: e.memset(self.z65[:, 0:64], 0.0))
        P.op("pool", lambda e: e.memset(self.z65[:, 64:65], 1.0), after=[tz])
        t6 = P.dma("pool", lambda e: e.dma_start(self.xtokp[:, 0:256].bitcast(F32),
                                                 self.ident_in[:, :]), after=[t4])
        P.op("dve", lambda e: e.tensor_copy(self.ident[:, :],
                                            self.xtokp[:, 0:256].bitcast(F32)), after=[t6])
        P.op("pool", lambda e: e.memset(self.cEPS[:, :], EPS), inc=False)
        P.op("pool", lambda e: e.memset(self.ones128[:, :], 1.0), inc=False)
        P.op("pool", lambda e: e.memset(self.onescol[:, :], 1.0), inc=False)
        P.op("pool", lambda e: e.memset(self.zrow[:, :], 0.0), inc=False)
        P.barrier()

    def prep_ffn_weights(self, barrier=True):
        P, nc, ar = self.P, self.nc, self.ar
        ar.reset()
        NB = 4
        sI = [ar.alloc(f"ffnI{i}", [128, 1024], I32) for i in range(NB)]
        sB = [ar.alloc(f"ffnB{i}", [128, 1024], BF16) for i in range(NB)]
        sRow = [ar.alloc(f"ffnR{i}", [128, 1024], BF16) for i in range(2)]
        ptr = [self.ps[:, 512 * i:512 * i + 64].bitcast(BF16) for i in (4, 5)]
        jobs = []
        for (w, dwT) in ((self.Wg, self.dWgT), (self.Wu, self.dWuT)):
            for r in range(HID // 128):
                jobs.append((w[128 * r:128 * r + 128, :],
                             dwT[:, 1024 * r:1024 * r + 1024]))
        for r in range(D // 128):
            for cc in range(3):
                jobs.append((self.Wd[128 * r:128 * r + 128, 1024 * cc:1024 * cc + 1024],
                             self.dWdT[:, 3072 * r + 1024 * cc:3072 * r + 1024 * cc + 1024]))
        tcast = {}
        tuse = {}
        tstore = {}
        tevac = {}
        for j, (src, dst) in enumerate(jobs):
            bi = j % NB
            ri = j % 2
            tl = P.dma(("sp", "pool")[j % 2], lambda e, src=src, bi=bi: e.dma_start(
                sI[bi][:, :], src), after=[tcast.get(bi)])
            tc = P.op("act", lambda e, bi=bi: e.activation(
                sB[bi][:, :], sI[bi][:, :], AF.Copy, scale=1.0 / 127.0),
                after=[tl, tuse.get(bi)])
            tcast[bi] = tc
            evs = []
            for c in range(8):
                tpi = (j * 8 + c) % 2
                tt = P.op("pe", lambda e, bi=bi, c=c, tpi=tpi: e.transpose(
                    ptr[tpi][:, 0:128], sB[bi][:, 128 * c:128 * c + 128],
                    self.ident[:, :]), after=[tc, tevac.get(tpi)])
                tuse[bi] = tt
                te = P.op("dve", lambda e, ri=ri, c=c, tpi=tpi: e.tensor_copy(
                    sRow[ri][:, 128 * c:128 * c + 128], ptr[tpi][:, 0:128]),
                    after=[tt, tstore.get(ri) if c == 0 else None])
                tevac[tpi] = te
                evs.append(te)
            ts = P.dma(("pool", "sp")[j % 2], lambda e, dst=dst, ri=ri: e.dma_start(
                dst, sRow[ri][:, :]), after=evs)
            tstore[ri] = ts
        if barrier:
            P.barrier()

    def prep_M(self, barrier=True, ar_off=0):
        P, nc, ar = self.P, self.nc, self.ar
        ar.reset(ar_off)
        sC = ar.alloc("sC", [128, NCI * D], BF16)
        sS = ar.alloc("sS", [128, NCI * D], BF16)
        sWr = ar.alloc("sWr", [128, NCI * D], BF16)
        sWi = ar.alloc("sWi", [128, NCI * D], BF16)
        sT1r = ar.alloc("sT1r", [128, NCI * D], BF16)
        sT1i = ar.alloc("sT1i", [128, NCI * D], BF16)
        self._sT1r = sT1r; self._sT1i = sT1i; self._sC = sC; self._sS = sS; self._sWr = sWr
        stg = [ar.alloc(f"mstage{i}", [128, 2048], F32) for i in range(2)]
        loads = [(self.Ctab, [(sC, 1.0)]), (self.Stab, [(sS, 1.0)]),
                 (self.wfr, [(sWr, 1.0)]), (self.wfi, [(sWi, 1.0)])]
        tcast = {}
        i = 0
        for (src, dsts) in loads:
            sr = src.rearrange("(c p) o -> p c o", p=128)
            for q in range(4):
                pi = i % 2
                tl = P.dma(("sp", "pool")[i % 2], lambda e, sr=sr, q=q, pi=pi: e.dma_start(
                    stg[pi][:, :], sr[:, 2 * q:2 * q + 2, :]), after=[tcast.get(pi)])
                tprev = None
                for (dst, sc) in dsts:
                    tprev = P.op("act", lambda e, dst=dst, q=q, sc=sc, pi=pi: e.activation(
                        dst[:, 2048 * q:2048 * (q + 1)], stg[pi][:, :], AF.Copy, scale=sc),
                        after=[tl, tprev])
                tcast[pi] = tprev
                i += 1
        load_toks = [tcast[0], tcast[1]]

        chain_guard = {}

        def mm_chain(dst, terms, scale, deps):
            tprev_ev = chain_guard
            evacs = []
            for oc in range(NCI):
                for mt in range(2):
                    bank = (oc * 2 + mt) % 2
                    pb = self.pbank(bank)
                    tmm = None
                    n_mm = len(terms) * NCI
                    imm = 0
                    for (lh, rh) in terms:
                        for ci in range(NCI):
                            first = (imm == 0)
                            last = (imm == n_mm - 1)
                            tmm = P.op("pe", lambda e, lh=lh, rh=rh, ci=ci, oc=oc, mt=mt,
                                       first=first, last=last, pb=pb: e.matmul(
                                pb[:, :],
                                lh[:, D * ci + 128 * oc: D * ci + 128 * oc + 128],
                                rh[:, D * ci + 512 * mt: D * ci + 512 * mt + 512],
                                start=first, stop=last),
                                after=[tprev_ev.get(bank)] + (deps if first else []))
                            imm += 1
                    te = P.op("act", lambda e, dst=dst, oc=oc, mt=mt, pb=pb, scale=scale:
                              e.activation(dst[:, D * oc + 512 * mt: D * oc + 512 * mt + 512],
                                           pb[:, :], AF.Copy, scale=scale), after=[tmm])
                    tprev_ev[bank] = te
                    evacs.append(te)
            return evacs

        ev1 = mm_chain(sT1r, [(sWr, sC), (sWi, sS)], 1.0, load_toks)
        tneg = []
        for q in range(4):
            tneg.append(P.op("act", lambda e, q=q: e.activation(
                sS[:, 2048 * q:2048 * q + 2048], sS[:, 2048 * q:2048 * q + 2048],
                AF.Copy, scale=-1.0), after=ev1 if q == 0 else []))
        ev1 += mm_chain(sT1i, [(sWi, sC), (sWr, sS)], 1.0, load_toks + tneg)
        mm_chain(self.M_bf, [(sT1r, sC), (sT1i, sS)], 1.0 / D, ev1)
        if barrier:
            P.barrier()

    def load_x(self, b, src=None, feature_major_src=False, barrier=True):
        P, ar = self.P, self.ar
        src = self.x_in if src is None else src
        if feature_major_src:
            for q in range(8):
                tl = P.dma("sp", lambda e, q=q: e.dma_start(
                    self.stage32[:, :], src[b, :, 2048 * q:2048 * (q + 1)]))
                P.op("dve", lambda e, q=q: e.tensor_copy(
                    self.x_bf[:, 2048 * q:2048 * (q + 1)], self.stage32[:, :]), after=[tl])
            P.barrier()
            return
        ptr = [self.ps[:, 512 * i:512 * i + 64].bitcast(BF16) for i in (6, 7)]
        tcast = {}
        ttr = {}
        tev = {}
        for tt in range(NTT):
            pi = tt % 2
            tl = P.dma(("sp", "pool")[pi], lambda e, tt=tt, pi=pi: e.dma_start(
                self.stage32[:, 1024 * pi:1024 * pi + 1024],
                src[b, 128 * tt:128 * tt + 128, :]), after=[tcast.get(pi)])
            tc = P.op("act", lambda e, tt=tt, pi=pi: e.copy(
                self.xtokp[:, 1024 * pi:1024 * pi + 1024],
                self.stage32[:, 1024 * pi:1024 * pi + 1024]),
                after=[tl, ttr.get(pi)])
            tcast[pi] = tc
            for ci in range(NCI):
                tpi = (tt * NCI + ci) % 2
                t1 = P.op("pe", lambda e, pi=pi, ci=ci, tpi=tpi: e.transpose(
                    ptr[tpi][:, 0:128],
                    self.xtokp[:, 1024 * pi + 128 * ci: 1024 * pi + 128 * ci + 128],
                    self.ident[:, :]), after=[tc, tev.get(tpi)])
                ttr[pi] = t1
                tev[tpi] = P.op("dve", lambda e, tt=tt, ci=ci, tpi=tpi: e.tensor_copy(
                    self.x_bf[:, T * ci + 128 * tt: T * ci + 128 * tt + 128],
                    ptr[tpi][:, 0:128]), after=[t1])
        if barrier:
            P.barrier()

    def dump_x(self, b):
        P = self.P
        for q in range(8):
            tc = P.op("act", lambda e, q=q: e.copy(
                self.stage32[:, :], self.x_bf[:, 2048 * q:2048 * (q + 1)]))
            P.dma("sp", lambda e, q=q: e.dma_start(
                self.x_dbg[b, :, 2048 * q:2048 * (q + 1)], self.stage32[:, :]), after=[tc])
            P.barrier()

    def emit_proj(self, sW, evac, n_out_chunks=NCI, trange=(0, T), scale=None):
        P = self.P
        t0, t1 = trange
        ntq = (t1 - t0) // 512
        tprev = {}
        for co in range(n_out_chunks):
            for tq in range(ntq):
                bank = (co * ntq + tq) % 2
                pb = self.pbank(bank)
                tmm = None
                for ci in range(NCI):
                    tmm = P.op("pe", lambda e, ci=ci, co=co, tq=tq, pb=pb: e.matmul(
                        pb[:, :],
                        sW[:, D * ci + 128 * co: D * ci + 128 * co + 128],
                        self.x_bf[:, T * ci + t0 + 512 * tq: T * ci + t0 + 512 * tq + 512],
                        start=(ci == 0), stop=(ci == NCI - 1)),
                        after=[tprev.get(bank)])
                tprev[bank] = evac(co, tq, pb, tmm)

    def l0_attn(self, b):
        P, ar, nc = self.P, self.ar, self.nc
        ar.reset()
        sQ = ar.alloc("sQ", [128, NCI * T], BF16)
        sK = ar.alloc("sK", [128, NCI * T], BF16)
        sV = ar.alloc("sV", [128, NTT * NH * 65], BF16)
        bufA = ar.alloc("wbufA", [128, NCI * D], BF16)
        bufB = ar.alloc("wbufB", [128, NCI * D], BF16)
        sPT = [ar.alloc(f"sPT2{p}", [128, 1024], BF16) for p in range(2)]
        sPM2 = [ar.alloc(f"sPM0m{i}", [128, 128], BF16) for i in range(4)]
        nsb = ar.alloc("nsb0", [128, 512], F32)
        num_sb = [nsb[0:64, :], nsb[64:128, :]]
        rden_sb = [ar.alloc(f"rd0{h}", [1, 512], BF16) for h in range(2)]
        self._attn_common(b, sQ, sK, sV, bufA, bufB, sPT, sPM2,
                          num_sb, rden_sb)
        ar.reset()
        self._ln(b)

    def _attn_common(self, b, sQ, sK, sV, bufA, bufB, sPT, sPM2,
                     num_sb, rden_sb):
        P = self.P
        tq_done = self._proj_qkv_l0(bufA, bufB, sQ, sK, sV)
        tones = P.op("pool", lambda e: e.memset(
            sV.ap().rearrange("p (n c) -> p n c", c=65)[:, :, 64:65], 1.0), after=[tq_done])
        P.barrier()

        prs = [(0, 64), (64, 128)]
        pt_readers = {p: [] for p in range(2)}
        exp_tok = {}
        pm_guard = {}
        bank_guard = {}
        num_guard = {}
        rden_guard = {}
        rf_guard = {}
        acc = [bufB[:, 0:1024].bitcast(F32), bufB[:, 1024:2048].bitcast(F32)]
        acc_bf = [bufB[:, 4096:4608], bufB[:, 4608:5120]]
        den_guard = {}
        accbf_guard = {}
        inst = 0
        self._pend1 = None
        self._pend2 = None
        for hp in range(NH // 2):
            for qq in range(4):
                par = inst % 2
                last_mm = {}
                first_mm = {0: True, 1: True}
                den_first = {0: True, 1: True}
                den_prev = {}
                prev = None

                def emit_av(ctx, par=par, hp=hp, qq=qq, last_mm=last_mm,
                            first_mm=first_mm, den_first=den_first,
                            den_prev=den_prev):
                    kt, tE, gidx = ctx
                    for h in range(2):
                        hh = 2 * hp + h
                        bi = 4 + 2 * par + h
                        nb = self.pnum(par, h)
                        block = sPT[gidx % 2][:, 512 * h: 512 * h + 512]
                        vs = 65 * (NH * kt + hh)
                        if DEN_DVE and kt >= 4 * qq:
                            w = min(128 * (kt - 4 * qq + 1), 512)
                            if den_first[h]:
                                tD = P.op("dve", lambda e, h=h, block=block:
                                          e.tensor_copy(acc[h], block),
                                          after=[tE, den_guard.get(h)])
                                den_first[h] = False
                            else:
                                tD = P.op("dve", lambda e, h=h, block=block, w=w:
                                          e.tensor_tensor(
                                    acc[h][:, 0:w], acc[h][:, 0:w],
                                    block[:, 0:w], op=ALU.add),
                                    after=[tE, den_prev[h]])
                            den_prev[h] = tD
                            pt_readers[gidx % 2].append(tD)
                        if (not DEN_DVE) and kt >= 4 * qq:
                            w = min(128 * (kt - 4 * qq + 1), 512)
                            bi_ = 4 + 2 * par + h
                            fd = [bank_guard.get((par, h))] if first_mm[h] else []
                            t3 = P.op("pe", lambda e, bi_=bi_, block=block, w=w,
                                      st_f=first_mm[h]: e.matmul(
                                self.ps[0:65, 512 * bi_: 512 * bi_ + w],
                                self.z65[:, :], block[:, 0:w],
                                start=st_f, stop=False), after=[tE] + fd)
                            first_mm[h] = False
                            pt_readers[gidx % 2].append(t3)
                            last_mm[h] = t3
                        if kt > 4 * qq + 3:
                            continue
                        if kt >= 4 * qq:
                            o = kt - 4 * qq
                            sl = (h, o % 2)
                            if o < 3:
                                t2 = P.op("pe", lambda e, bi=bi, block=block,
                                          vs=vs, o=o: e.matmul(
                                    self.ps[0:65, 512 * bi + 128 * (o + 1):
                                            512 * bi + 512],
                                    sV[:, vs:vs + 65],
                                    block[:, 128 * (o + 1): 512],
                                    start=False, stop=False), after=[tE])
                                pt_readers[gidx % 2].append(t2)
                            tMk = P.op("dve", lambda e, h=h, o=o, block=block:
                                       e.tensor_tensor(
                                sPM2[2 * h + o % 2][:, :],
                                block[:, 128 * o: 128 * o + 128],
                                self.sMask[:, :], op=ALU.mult),
                                after=[tE, pm_guard.get(sl)])
                            pt_readers[gidx % 2].append(tMk)
                            fdeps = [bank_guard.get((par, h))] if first_mm[h] else []
                            t1 = P.op("pe", lambda e, bi=bi, h=h, o=o, vs=vs,
                                      st_f=first_mm[h],
                                      sp_f=(kt == 0 and not DEN_DVE): e.matmul(
                                self.ps[0:64, 512 * bi + 128 * o:
                                        512 * bi + 128 * o + 128],
                                sV[:, vs:vs + 64],
                                sPM2[2 * h + o % 2][:, :],
                                start=st_f, stop=sp_f), after=[tMk] + fdeps)
                            first_mm[h] = False
                            pm_guard[sl] = t1
                            last_mm[h] = t1
                            if DEN_DVE and o == 0:
                                tC = P.op("dve", lambda e, h=h: e.tensor_copy(
                                    acc_bf[h], acc[h]),
                                    after=[den_prev[h], accbf_guard.get(h)])
                                den_guard[h] = tC
                                tCS = P.op("pe", lambda e, nb=nb, h=h,
                                           stq=(qq == 0): e.matmul(
                                    nb, self.z65[:, :], acc_bf[h],
                                    start=False, stop=stq), after=[tC])
                                accbf_guard[h] = tCS
                                if qq == 0:
                                    last_mm[h] = tCS
                        else:
                            stop_f = (kt == 0)
                            t1 = P.op("pe", lambda e, nb=nb, block=block, vs=vs,
                                      stop_f=stop_f: e.matmul(
                                nb, sV[:, vs:vs + 65], block,
                                start=False, stop=stop_f), after=[tE])
                            pt_readers[gidx % 2].append(t1)
                            last_mm[h] = t1

                for g in range(NTT):
                    kt = 15 - g
                    par2 = g % 2
                    tSs = []
                    for h in range(2):
                        pr = prs[h]
                        pSb = self.ps[:, 1024 * par2 + 512 * h:
                                      1024 * par2 + 512 * h + 512]
                        tS = P.op("pe", lambda e, hp=hp, h=h, kt=kt, qq=qq, pr=pr,
                                  pSb=pSb: e.matmul(
                            pSb,
                            sK[pr[0]:pr[1], T * hp + 128 * kt: T * hp + 128 * kt + 128],
                            sQ[pr[0]:pr[1], T * hp + 512 * qq: T * hp + 512 * qq + 512],
                            start=True, stop=True),
                            after=[exp_tok.get(par2)])
                        tSs.append(tS)
                    tE = P.op("act", lambda e, par2=par2: e.activation(
                        sPT[par2][:, :], self.ps[:, 1024 * par2: 1024 * par2 + 1024],
                        AF.Exp, scale=QSCALE),
                        after=tSs + pt_readers[par2])
                    pt_readers[par2] = []
                    exp_tok[par2] = tE
                    if prev is not None:
                        emit_av(prev)
                    if g == 1 and self._pend1 is not None:
                        self._pend1()
                        self._pend1 = None
                    if g == 3 and self._pend2 is not None:
                        self._pend2()
                        self._pend2 = None
                    prev = (kt, tE, g)
                emit_av(prev)

                def fin1(par=par, hp=hp, qq=qq, last_mm=last_mm):
                    toks = {}
                    for h in range(2):
                        bi = 4 + 2 * par + h
                        tLn = P.op("act", lambda e, bi=bi: e.activation(
                            self.ps[96:97, 512 * bi: 512 * bi + 512],
                            self.ps[64:65, 512 * bi: 512 * bi + 512], AF.Ln),
                            after=[last_mm[h], rf_guard.get((par, h))])
                        tR = P.op("act", lambda e, h=h, bi=bi: e.activation(
                            rden_sb[h][:, :],
                            self.ps[96:97, 512 * bi: 512 * bi + 512],
                            AF.Exp, scale=-1.0),
                            after=[tLn, rden_guard.get(h)])
                        rf_guard[(par, h)] = tR
                        tEv = P.op("dve", lambda e, h=h, bi=bi: e.tensor_copy(
                            num_sb[h],
                            self.ps[0:64, 512 * bi: 512 * bi + 512]),
                            after=[last_mm[h], num_guard.get(h)])
                        toks[h] = (tR, tEv)

                    def fin2():
                        for h in range(2):
                            bi = 4 + 2 * par + h
                            tR, tEv = toks[h]
                            tB = P.op("pe", lambda e, h=h, bi=bi: e.matmul(
                                self.ps[0:64, 512 * bi: 512 * bi + 512],
                                self.ones128[0:1, 0:64], rden_sb[h][:, :],
                                start=True, stop=True), after=[tR, tEv])
                            rden_guard[h] = tB
                            tM = P.op("dve", lambda e, h=h, bi=bi:
                                      e.tensor_tensor(
                                num_sb[h], num_sb[h],
                                self.ps[0:64, 512 * bi: 512 * bi + 512],
                                op=ALU.mult), after=[tB])
                            bank_guard[(par, h)] = tM
                            dst = self.x_bf[64 * h:64 * h + 64,
                                            T * hp + 512 * qq: T * hp + 512 * qq + 512]
                            tX = P.op("dve", lambda e, h=h, dst=dst:
                                      e.tensor_tensor(
                                dst, dst, num_sb[h],
                                op=ALU.add), after=[tM])
                            num_guard[h] = tX
                    self._pend2 = fin2
                self._pend1 = fin1
                inst += 1
        if self._pend1 is not None:
            self._pend1()
            self._pend1 = None
        if self._pend2 is not None:
            self._pend2()
            self._pend2 = None
        P.barrier()


    def _proj_qkv_l0(self, bufA, bufB, sQ, sK, sV):
        P = self.P

        def evac_q(dst):
            def f(co, tq, pb, tok):
                return P.op("act", lambda e, co=co, tq=tq, pb=pb, dst=dst: e.copy(
                    dst[:, T * co + 512 * tq: T * co + 512 * tq + 512], pb[:, :]), after=[tok])
            return f
        pbg = {}
        tldA = [P.dma("sp", lambda e: e.dma_start(bufA[:, 0:4096], self.dWq0[:, 0:4096])),
                P.dma("pool", lambda e: e.dma_start(bufA[:, 4096:8192], self.dWq0[:, 4096:8192]))]
        tldB = [P.dma("sp", lambda e: e.dma_start(bufB[:, 0:4096], self.dWk0[:, 0:4096])),
                P.dma("pool", lambda e: e.dma_start(bufB[:, 4096:8192], self.dWk0[:, 4096:8192]))]
        tq_use = self.emit_proj_w(bufA, evac_q(sQ), tldA, pbg)
        tldV = [P.dma("sp", lambda e: e.dma_start(bufA[:, 0:4096], self.dWv0[:, 0:4096]), after=[tq_use]),
                P.dma("pool", lambda e: e.dma_start(bufA[:, 4096:8192], self.dWv0[:, 4096:8192]), after=[tq_use])]
        self.emit_proj_w(bufB, evac_q(sK), tldB, pbg)
        tprev = pbg
        tlast = None
        for tt in range(NTT):
            for nt in range(2):
                bank = (tt * 2 + nt) % 2
                pb = self.pbank(bank)
                tmm = None
                for ci in range(NCI):
                    tmm = P.op("pe", lambda e, ci=ci, tt=tt, nt=nt, pb=pb: e.matmul(
                        pb[:, :],
                        self.x_bf[:, T * ci + 128 * tt: T * ci + 128 * tt + 128],
                        bufA[:, D * ci + 512 * nt: D * ci + 512 * nt + 512],
                        start=(ci == 0), stop=(ci == NCI - 1)),
                        after=[tprev.get(bank)] +
                              (tldV if tt == 0 and nt == 0 else []))
                te = None
                for hh in range(8 * nt, 8 * nt + 8):
                    te = P.op("act", lambda e, tt=tt, hh=hh, nt=nt, pb=pb: e.copy(
                        sV[:, 65 * (NH * tt + hh): 65 * (NH * tt + hh) + 64],
                        pb[:, 64 * (hh - 8 * nt): 64 * (hh - 8 * nt) + 64]), after=[tmm])
                tprev[bank] = te
                tlast = te
        return tlast

    def emit_proj_w(self, wbuf, evac, wait_tok, pbg, src=None):
        P = self.P
        if src is None:
            src = self.x_bf
        if not isinstance(wait_tok, list):
            wait_tok = [wait_tok]
        tmm = None
        for co in range(NCI):
            for tq in range(4):
                bank = (co * 4 + tq) % 2
                pb = self.pbank(bank)
                tmm = None
                for ci in range(NCI):
                    tmm = P.op("pe", lambda e, ci=ci, co=co, tq=tq, pb=pb, src=src: e.matmul(
                        pb[:, :],
                        wbuf[:, D * ci + 128 * co: D * ci + 128 * co + 128],
                        src[:, T * ci + 512 * tq: T * ci + 512 * tq + 512],
                        start=(ci == 0), stop=(ci == NCI - 1)),
                        after=[pbg.get(bank)] +
                              (wait_tok if co == 0 and tq == 0 and ci == 0 else []))
                pbg[bank] = evac(co, tq, pb, tmm)
        return tmm


    def l0_ffn(self, b, skip_ln=False):
        P, ar = self.P, self.ar
        ar.reset()
        h_bf = ar.alloc("h_bf", [128, NHC * T], BF16)
        self._h_bf = h_bf
        wgc = [ar.alloc(f"wgc{i}", [128, NCI * 128], BF16) for i in range(2)]
        wuc = [ar.alloc(f"wuc{i}", [128, NCI * 128], BF16) for i in range(2)]
        wdc = [ar.alloc(f"wdc{i}", [128, NHC * 128], BF16) for i in range(2)]
        sSil = [ar.alloc(f"sSil{i}", [128, 1024], BF16) for i in range(2)]

        g_use = {}
        sil_use = {}
        bank_use = {}
        hdone = {}
        for hc in range(NHC):
            wb = hc % 2
            tlg = P.dma("sp", lambda e, hc=hc, wb=wb: e.dma_start(
                wgc[wb][:, :], self.dWgT[:, 1024 * hc:1024 * hc + 1024]), after=[g_use.get(wb)])
            tlu = P.dma("pool", lambda e, hc=hc, wb=wb: e.dma_start(
                wuc[wb][:, :], self.dWuT[:, 1024 * hc:1024 * hc + 1024]), after=[g_use.get(wb)])
            for th in range(2):
                pp = (hc * 2 + th) % 2
                pG = self.ps[:, 1024 * pp:1024 * pp + 1024]
                pU = self.ps[:, 2048 + 1024 * pp:2048 + 1024 * pp + 1024]
                tg = tu = None
                for sub in range(2):
                    for ci in range(NCI):
                        tg = P.op("pe", lambda e, ci=ci, th=th, sub=sub, wb=wb, pG=pG: e.matmul(
                            pG[:, 512 * sub:512 * sub + 512],
                            wgc[wb][:, 128 * ci:128 * ci + 128],
                            self.x_bf[:, T * ci + 1024 * th + 512 * sub:
                                      T * ci + 1024 * th + 512 * sub + 512],
                            start=(ci == 0), stop=(ci == NCI - 1)),
                            after=[tlg, bank_use.get(("G", pp))])
                for sub in range(2):
                    for ci in range(NCI):
                        tu = P.op("pe", lambda e, ci=ci, th=th, sub=sub, wb=wb, pU=pU: e.matmul(
                            pU[:, 512 * sub:512 * sub + 512],
                            wuc[wb][:, 128 * ci:128 * ci + 128],
                            self.x_bf[:, T * ci + 1024 * th + 512 * sub:
                                      T * ci + 1024 * th + 512 * sub + 512],
                            start=(ci == 0), stop=(ci == NCI - 1)),
                            after=[tlu, bank_use.get(("U", pp))])
                g_use[wb] = tu
                tsil = P.op("act", lambda e, pp=pp, pG=pG: e.activation(
                    sSil[pp][:, :], pG[:, :], AF.Silu), after=[tg, sil_use.get(pp)])
                thm = P.op("dve", lambda e, hc=hc, th=th, pp=pp, pU=pU: e.tensor_tensor(
                    h_bf[:, T * hc + 1024 * th: T * hc + 1024 * th + 1024],
                    sSil[pp][:, :], pU[:, :], op=ALU.mult), after=[tu, tsil])
                sil_use[pp] = thm
                bank_use[("G", pp)] = tsil
                bank_use[("U", pp)] = thm
                hdone[(hc, th)] = thm
        d_use = {}
        dbank_use = {}
        for co in range(NCI):
            wb = co % 2
            tld = P.dma(("sp", "pool")[co % 2], lambda e, co=co, wb=wb: e.dma_start(
                wdc[wb][:, :], self.dWdT[:, 3072 * co:3072 * co + 3072]), after=[d_use.get(wb)])
            tmm = None
            for tq in range(4):
                pp = (co * 4 + tq) % 2
                pD = self.pbank(pp)
                deps = [tld, dbank_use.get(pp)] + [hdone[(hc, tq // 2)] for hc in range(NHC)]
                tmm = None
                for hc in range(NHC):
                    tmm = P.op("pe", lambda e, hc=hc, co=co, tq=tq, wb=wb, pD=pD: e.matmul(
                        pD[:, :],
                        wdc[wb][:, 128 * hc:128 * hc + 128],
                        h_bf[:, T * hc + 512 * tq: T * hc + 512 * tq + 512],
                        start=(hc == 0), stop=(hc == NHC - 1)),
                        after=deps if hc == 0 else [])
                tev = P.op("dve", lambda e, co=co, tq=tq, pD=pD: e.tensor_tensor(
                    self.x_bf[:, T * co + 512 * tq: T * co + 512 * tq + 512],
                    self.x_bf[:, T * co + 512 * tq: T * co + 512 * tq + 512],
                    pD[:, :], op=ALU.add), after=[tmm])
                dbank_use[pp] = tev
            d_use[wb] = tmm
        P.barrier()
        ar.reset()
        if not skip_ln:
            self._ln(b)

    def l1_attn(self, b):
        P, ar = self.P, self.ar
        ar.reset()
        sQ = ar.alloc("sQ1", [128, NCI * T], BF16)
        sAO = ar.alloc("sAO", [128, NCI * T], BF16)
        kv2 = ar.alloc("kv2", [128, T], BF16)
        sKVtok = ar.alloc("sKVtok", [128, NTT * 65], BF16)
        bufA = ar.alloc("wbufA1", [128, NCI * D], BF16)
        bufB = ar.alloc("wbufB1", [128, NCI * D], BF16)
        sPT = [ar.alloc(f"sPT1{p}", [128, 1024], BF16) for p in range(2)]
        sPM2 = [ar.alloc(f"sPM1m{i}", [128, 128], BF16) for i in range(4)]
        num_sb = [[ar.alloc(f"nsb1{p}{h}", [64, 512], F32) for h in range(2)]
                  for p in range(2)]
        rden_f32 = [[ar.alloc(f"rf1{p}{h}", [1, 512], F32) for h in range(2)]
                    for p in range(2)]
        rden_sb = [[ar.alloc(f"rd1{p}{h}", [1, 512], BF16) for h in range(2)]
                   for p in range(2)]
        sPnAll = [ar.alloc(f"sPnA{h}", [1, T], BF16) for h in range(2)]

        pbg = {}
        tldA = [P.dma("sp", lambda e: e.dma_start(bufA[:, 0:4096], self.dWq1[:, 0:4096])),
                P.dma("pool", lambda e: e.dma_start(bufA[:, 4096:8192], self.dWq1[:, 4096:8192]))]
        tldB = [P.dma("sp", lambda e: e.dma_start(bufB[:, 0:4096], self.dWout[:, 0:4096])),
                P.dma("pool", lambda e: e.dma_start(bufB[:, 4096:8192], self.dWout[:, 4096:8192]))]

        def evac_q(co, tq, pb, tok):
            return P.op("act", lambda e, co=co, tq=tq, pb=pb: e.copy(
                sQ[:, T * co + 512 * tq: T * co + 512 * tq + 512], pb[:, :]), after=[tok])
        self.emit_proj_w(bufA, evac_q, tldA, pbg)

        kv_evac = []
        for tq in range(4):
            bank = tq % 2
            pb = self.pbank(bank)
            tmm = None
            for ci in range(NCI):
                tmm = P.op("pe", lambda e, ci=ci, tq=tq, pb=pb: e.matmul(
                    pb[0:64, :], self.sWkv[:, HD * ci:HD * ci + HD],
                    self.x_bf[:, T * ci + 512 * tq: T * ci + 512 * tq + 512],
                    start=(ci == 0), stop=(ci == NCI - 1)),
                    after=[pbg.get(bank)] if ci == 0 else [])
            te = P.op("act", lambda e, tq=tq, pb=pb: e.copy(
                kv2[0:64, 512 * tq:512 * tq + 512], pb[0:64, :]), after=[tmm])
            pbg[bank] = te
            kv_evac.append(te)
        tdup = P.dma("pool", lambda e: e.dma_start(kv2[64:128, :], kv2[0:64, :]),
                     after=kv_evac)
        for tt in range(NTT):
            bank = tt % 2
            pb = self.pbank(bank)
            tmm = None
            for ci in range(NCI):
                tmm = P.op("pe", lambda e, ci=ci, tt=tt, pb=pb: e.matmul(
                    pb[:, 0:64],
                    self.x_bf[:, T * ci + 128 * tt: T * ci + 128 * tt + 128],
                    self.sWkv[:, HD * ci:HD * ci + HD],
                    start=(ci == 0), stop=(ci == NCI - 1)),
                    after=[pbg.get(bank)] if ci == 0 else [])
            te = P.op("act", lambda e, tt=tt, pb=pb: e.copy(
                sKVtok[:, 65 * tt:65 * tt + 64], pb[:, 0:64]), after=[tmm])
            pbg[bank] = te
        tones = P.op("pool", lambda e: e.memset(
            sKVtok.ap().rearrange("p (n c) -> p n c", c=65)[:, :, 64:65], 1.0),
            after=[P.last("act")])
        P.barrier()

        prs = [(0, 64), (64, 128)]
        pt_readers = {p: [] for p in range(2)}
        exp_tok = {}
        null_read = {0: [], 1: []}
        pm_guard = {}
        bank_guard = {}
        num_guard = {}
        rden_guard = {}
        rf_guard = {}
        inst = 0
        self._pend1 = None
        self._pend2 = None
        for hp in range(NH // 2):
            tEn = {}
            all_exp = [exp_tok.get(p) for p in range(2)]
            for h in range(2):
                pr = prs[h]
                pnq = []
                for q4 in range(4):
                    pnq.append(P.op("pe", lambda e, hp=hp, h=h, q4=q4, pr=pr: e.matmul(
                        self.ps[0:1, 512 * q4: 512 * q4 + 512],
                        self.nk_col[pr[0]:pr[1], 0:1],
                        sQ[pr[0]:pr[1], T * hp + 512 * q4: T * hp + 512 * q4 + 512],
                        start=True, stop=True),
                        after=all_exp + [tEn.get(0)]))
                tEn[h] = P.op("act", lambda e, h=h: e.activation(
                    sPnAll[h][:, :], self.ps[0:1, 0:2048],
                    AF.Exp, scale=QSCALE), after=pnq + null_read[h])
                null_read[h] = []
            for p in range(2):
                exp_tok[p] = tEn[1]
            for qq in range(4):
                par = inst % 2
                last_mm = {}
                for h in range(2):
                    nb = self.pnum(par, h)
                    tN = P.op("pe", lambda e, nb=nb, h=h, qq=qq: e.matmul(
                        nb, self.nkv65[0:1, :],
                        sPnAll[h][0:1, 512 * qq: 512 * qq + 512],
                        start=True, stop=False),
                        after=[tEn[h], bank_guard.get((par, h))])
                    null_read[h].append(tN)
                prev = None

                def emit_av(ctx, par=par, hp=hp, qq=qq, last_mm=last_mm):
                    kt, tE, gidx = ctx
                    for h in range(2):
                        bi = 4 + 2 * par + h
                        nb = self.pnum(par, h)
                        block = sPT[gidx % 2][:, 512 * h: 512 * h + 512]
                        vs = 65 * kt
                        stop_f = (kt == 0)
                        if kt >= 4 * qq:
                            o = kt - 4 * qq
                            sl = (h, o % 2)
                            if o < 3:
                                t2 = P.op("pe", lambda e, bi=bi, block=block,
                                          vs=vs, o=o: e.matmul(
                                    self.ps[0:65, 512 * bi + 128 * (o + 1):
                                            512 * bi + 512],
                                    sKVtok[:, vs:vs + 65],
                                    block[:, 128 * (o + 1): 512],
                                    start=False, stop=False), after=[tE])
                                pt_readers[gidx % 2].append(t2)
                            tMk = P.op("dve", lambda e, h=h, o=o, block=block:
                                       e.tensor_tensor(
                                sPM2[2 * h + o % 2][:, :],
                                block[:, 128 * o: 128 * o + 128],
                                self.sMask[:, :], op=ALU.mult),
                                after=[tE, pm_guard.get(sl)])
                            pt_readers[gidx % 2].append(tMk)
                            t1 = P.op("pe", lambda e, bi=bi, h=h, o=o, vs=vs,
                                      stop_f=stop_f: e.matmul(
                                self.ps[0:65, 512 * bi + 128 * o:
                                        512 * bi + 128 * o + 128],
                                sKVtok[:, vs:vs + 65],
                                sPM2[2 * h + o % 2][:, :],
                                start=False, stop=stop_f), after=[tMk])
                            pm_guard[sl] = t1
                        else:
                            t1 = P.op("pe", lambda e, nb=nb, block=block, vs=vs,
                                      stop_f=stop_f: e.matmul(
                                nb, sKVtok[:, vs:vs + 65], block,
                                start=False, stop=stop_f), after=[tE])
                            pt_readers[gidx % 2].append(t1)
                        last_mm[h] = t1

                nkt = 4 * qq + 4
                for g in range(nkt):
                    kt = nkt - 1 - g
                    par2 = g % 2
                    tSs = []
                    for h in range(2):
                        pr = prs[h]
                        pSb = self.ps[:, 1024 * par2 + 512 * h:
                                      1024 * par2 + 512 * h + 512]
                        tS = P.op("pe", lambda e, h=h, kt=kt, qq=qq, hp=hp, pr=pr,
                                  pSb=pSb: e.matmul(
                            pSb,
                            kv2[pr[0]:pr[1], 128 * kt: 128 * kt + 128],
                            sQ[pr[0]:pr[1], T * hp + 512 * qq: T * hp + 512 * qq + 512],
                            start=True, stop=True),
                            after=[exp_tok.get(par2)])
                        tSs.append(tS)
                    tE = P.op("act", lambda e, par2=par2: e.activation(
                        sPT[par2][:, :], self.ps[:, 1024 * par2: 1024 * par2 + 1024],
                        AF.Exp, scale=QSCALE),
                        after=tSs + pt_readers[par2])
                    pt_readers[par2] = []
                    exp_tok[par2] = tE
                    if prev is not None:
                        emit_av(prev)
                    if g == 1 and self._pend1 is not None:
                        self._pend1()
                        self._pend1 = None
                    if g == 3 and self._pend2 is not None:
                        self._pend2()
                        self._pend2 = None
                    prev = (kt, tE, g)
                emit_av(prev)

                def fin1(par=par, hp=hp, qq=qq, last_mm=last_mm):
                    toks = {}
                    for h in range(2):
                        bi = 4 + 2 * par + h
                        tLn = P.op("act", lambda e, par=par, h=h, bi=bi:
                                   e.activation(
                            rden_f32[par][h][:, :],
                            self.ps[64:65, 512 * bi: 512 * bi + 512], AF.Ln),
                            after=[last_mm[h], rf_guard.get((par, h))])
                        tR = P.op("act", lambda e, par=par, h=h: e.activation(
                            rden_sb[par][h][:, :], rden_f32[par][h][:, :],
                            AF.Exp, scale=-1.0),
                            after=[tLn, rden_guard.get((par, h))])
                        rf_guard[(par, h)] = tR
                        tEv = P.op("dve", lambda e, par=par, h=h, bi=bi: e.tensor_copy(
                            num_sb[par][h][:, :],
                            self.ps[0:64, 512 * bi: 512 * bi + 512]),
                            after=[last_mm[h], num_guard.get((par, h))])
                        toks[h] = (tR, tEv)

                    def fin2():
                        for h in range(2):
                            bi = 4 + 2 * par + h
                            tR, tEv = toks[h]
                            tB = P.op("pe", lambda e, par=par, h=h, bi=bi: e.matmul(
                                self.ps[0:64, 512 * bi: 512 * bi + 512],
                                self.ones128[0:1, 0:64], rden_sb[par][h][:, :],
                                start=True, stop=True), after=[tR, tEv])
                            rden_guard[(par, h)] = tB
                            dst = sAO[64 * h:64 * h + 64,
                                      T * hp + 512 * qq: T * hp + 512 * qq + 512]
                            tM = P.op("dve", lambda e, par=par, h=h, bi=bi, dst=dst:
                                      e.tensor_tensor(
                                dst, num_sb[par][h][:, :],
                                self.ps[0:64, 512 * bi: 512 * bi + 512],
                                op=ALU.mult), after=[tB])
                            bank_guard[(par, h)] = tM
                            num_guard[(par, h)] = tM
                    self._pend2 = fin2
                self._pend1 = fin1
                inst += 1
        if self._pend1 is not None:
            self._pend1()
            self._pend1 = None
        if self._pend2 is not None:
            self._pend2()
            self._pend2 = None
        P.barrier()


        pbg = {}

        def evac_out(co, tq, pb, tok):
            return P.op("dve", lambda e, co=co, tq=tq, pb=pb: e.tensor_tensor(
                self.x_bf[:, T * co + 512 * tq: T * co + 512 * tq + 512],
                self.x_bf[:, T * co + 512 * tq: T * co + 512 * tq + 512],
                pb[:, :], op=ALU.add), after=[tok])
        self.emit_proj_w(bufB, evac_out, tldB, pbg, src=sAO)
        P.barrier()
        ar.reset()
        self._ln(b)

    def l1_ffn_final(self, b, next_b=None):
        P, ar = self.P, self.ar
        ar.reset()
        xNew = ar.alloc("xNew", [128, NCI * T], BF16)
        pbg = {}
        for co in range(NCI):
            for tq in range(4):
                bank = (co * 4 + tq) % 2
                pb = self.pbank(bank)
                tmm = None
                for ci in range(NCI):
                    tmm = P.op("pe", lambda e, ci=ci, co=co, tq=tq, pb=pb: e.matmul(
                        pb[:, :],
                        self.M_bf[:, D * ci + 128 * co: D * ci + 128 * co + 128],
                        self.x_bf[:, T * ci + 512 * tq: T * ci + 512 * tq + 512],
                        start=(ci == 0), stop=(ci == NCI - 1)),
                        after=[pbg.get(bank)] if ci == 0 else [])
                tev = P.op("dve", lambda e, co=co, tq=tq, pb=pb: e.scalar_tensor_tensor(
                    xNew[:, T * co + 512 * tq: T * co + 512 * tq + 512],
                    pb[:, :], self.sBias[:, co:co + 1],
                    self.x_bf[:, T * co + 512 * tq: T * co + 512 * tq + 512],
                    op0=ALU.add, op1=ALU.add), after=[tmm])
                pbg[bank] = tev
        P.barrier()
        self._ln(b, buf=xNew)
        if next_b is not None:
            self.load_x(next_b, barrier=False)
        yout = [ar.alloc(f"fyout{i}", [128, D], F32) for i in range(2)]
        ptr = [self.ps[:, 512 * i:512 * i + 64].bitcast(BF16) for i in range(2)]
        tr_use = {}
        ys_use = {}
        for tt in range(NTT):
            pi = tt % 2
            tevs = []
            for ci in range(NCI):
                tpi = (tt * NCI + ci) % 2
                t1 = P.op("pe", lambda e, ci=ci, tt=tt, tpi=tpi: e.transpose(
                    ptr[tpi][:, 0:128],
                    xNew[:, T * ci + 128 * tt: T * ci + 128 * tt + 128],
                    self.ident[:, :]), after=[tr_use.get(tpi)])
                te = P.op("dve", lambda e, ci=ci, pi=pi, tpi=tpi: e.tensor_copy(
                    yout[pi][:, 128 * ci:128 * ci + 128], ptr[tpi][:, 0:128]),
                    after=[t1, ys_use.get(pi) if ci == 0 else None])
                tr_use[tpi] = te
                tevs.append(te)
            ys_use[pi] = P.dma(("sp", "pool")[pi], lambda e, b=b, tt=tt, pi=pi:
                               e.dma_start(
                self.y_out[b, 128 * tt:128 * tt + 128, :], yout[pi][:, :]), after=tevs)
        P.barrier()

    def _ln(self, b, buf=None):
        P, ar = self.P, self.ar
        if buf is None:
            buf = self.x_bf
        off0 = ar.mark()
        sXsq = [ar.alloc(f"lnXsq{i}", [128, T], BF16) for i in range(2)]
        sM2 = ar.alloc("lnM2", [128, T], BF16)
        sVar = ar.alloc("lnVar", [128, T], BF16)
        sLn = ar.alloc("lnLn", [128, T], F32)
        sRstd = ar.alloc("lnRstd", [128, T], BF16)
        sMean = ar.alloc("lnMean", [128, T], BF16)
        sXm2 = ar.alloc("lnXm2", [128, T], BF16)
        sXm = ar.alloc("lnXm", [128, T], BF16)
        pA = self.ps[:, 0:2048]
        pB = self.ps[:, 2048:4096]
        tsq = {}
        tmmA = None
        tmmB = None
        for ci in range(NCI):
            pi = ci % 2
            tq_ = P.op("act", lambda e, ci=ci, pi=pi: e.activation(
                sXsq[pi][:, :], buf[:, T * ci:T * ci + T], AF.Square),
                after=[tsq.get(pi)])
            for t4 in range(4):
                tmmA = P.op("pe", lambda e, ci=ci, t4=t4: e.matmul(
                    pA[:, 512 * t4:512 * t4 + 512], self.ones128[:, :],
                    buf[:, T * ci + 512 * t4: T * ci + 512 * t4 + 512],
                    start=(ci == 0), stop=(ci == NCI - 1)), after=[])
                tmmB = P.op("pe", lambda e, ci=ci, t4=t4, pi=pi: e.matmul(
                    pB[:, 512 * t4:512 * t4 + 512], self.ones128[:, :],
                    sXsq[pi][:, 512 * t4:512 * t4 + 512],
                    start=(ci == 0), stop=(ci == NCI - 1)), after=[tq_])
            tsq[pi] = tmmB
        tm2 = P.op("act", lambda e: e.activation(sM2[:, :], pA[:, :], AF.Square,
                                                 scale=1.0 / D), after=[tmmA])
        tMean = P.op("act", lambda e: e.activation(sMean[:, :], pA[:, :], AF.Copy,
                                                   scale=1.0 / D), after=[tmmA])
        tv = P.op("dve", lambda e: e.scalar_tensor_tensor(
            sVar[:, :], pB[:, :], 1.0 / D, sM2[:, :], op0=ALU.mult, op1=ALU.subtract),
            after=[tmmB, tm2])
        tl_ = P.op("act", lambda e: e.activation(sLn[:, :], sVar[:, :], AF.Ln, bias=EPS),
                   after=[tv])
        tr_ = P.op("act", lambda e: e.activation(sRstd[:, :], sLn[:, :], AF.Exp, scale=-0.5),
                   after=[tl_])
        lastap = {}
        for ci in range(NCI):
            eng = "dve"
            sx = sXm if ci % 2 == 0 else sXm2
            txm = P.op(eng, lambda e, ci=ci, sx=sx: e.tensor_tensor(
                sx[:, :], buf[:, T * ci:T * ci + T], sMean[:, :],
                op=ALU.subtract), after=[tMean, lastap.get(eng)])
            lastap[eng] = P.op(eng, lambda e, ci=ci, sx=sx: e.tensor_tensor(
                buf[:, T * ci:T * ci + T], sx[:, :], sRstd[:, :], op=ALU.mult),
                after=[tr_, txm])
        P.barrier()
        ar.reset(off0)


from concourse.bass_utils import run_bass_kernel_spmd

N_CORES = 8
B_FULL = 32
B_CORE = B_FULL // N_CORES


def _host_consts():
    nk = np.arange(D)
    ang = 2.0 * np.pi * np.outer(nk, nk) / D
    Ctab = np.cos(ang).astype(np.float32)
    Stab = np.sin(ang).astype(np.float32)
    masks = (np.arange(128)[:, None] <= np.arange(128)[None, :]).astype(np.float32)
    ident = np.eye(128, dtype=np.float32)
    return Ctab, Stab, masks, ident


def _build():
    k = K(n_batch=B_CORE)
    k.prep_attn_weights()
    k.prep_ffn_weights(barrier=False)
    k.prep_M(barrier=False, ar_off=28672)
    k.load_x(0, barrier=False)
    k.P.barrier()
    for b in range(B_CORE):
        k.l0_attn(b)
        k.l0_ffn(b)
        k.l1_attn(b)
        k.l1_ffn_final(b, next_b=b + 1 if b + 1 < B_CORE else None)
    k.nc.finalize()
    return k


def kernel(L0_attn_content, L0_attn_Wq, L0_attn_Wk, L0_attn_Wv,
           L0_ffn_w_gate, L0_ffn_w_up, L0_ffn_w_down,
           L1_attn_Wq, L1_attn_Wkv, L1_attn_null_kv, L1_attn_Wout,
           L1_ffn_w_freq_real, L1_ffn_w_freq_imag, L1_ffn_bias):
    Ctab, Stab, masks, ident = _host_consts()
    x = np.ascontiguousarray(np.asarray(L0_attn_content, dtype=np.float32))
    shared = {
        "Wq0": np.asarray(L0_attn_Wq, np.float32),
        "Wk0": np.asarray(L0_attn_Wk, np.float32),
        "Wv0": np.asarray(L0_attn_Wv, np.float32),
        "Wg": np.asarray(L0_ffn_w_gate, np.int32),
        "Wu": np.asarray(L0_ffn_w_up, np.int32),
        "Wd": np.asarray(L0_ffn_w_down, np.int32),
        "Wq1": np.asarray(L1_attn_Wq, np.float32),
        "Wkv": np.asarray(L1_attn_Wkv, np.float32),
        "nullkv": np.asarray(L1_attn_null_kv, np.float32),
        "Wout": np.asarray(L1_attn_Wout, np.float32),
        "wfr": np.asarray(L1_ffn_w_freq_real, np.float32),
        "wfi": np.asarray(L1_ffn_w_freq_imag, np.float32),
        "bias1": np.asarray(L1_ffn_bias, np.float32),
        "Ctab": Ctab, "Stab": Stab, "masks_in": masks, "ident_in": ident,
    }
    k = _build()
    in_maps = []
    for c in range(N_CORES):
        m = dict(shared)
        m["x_in"] = x[B_CORE * c: B_CORE * (c + 1)]
        in_maps.append(m)
    res = run_bass_kernel_spmd(k.nc, in_maps, core_ids=list(range(N_CORES)))
    return np.concatenate([np.asarray(r["y_out"]) for r in res.results], axis=0)
